# version 11 (final)
# baseline (speedup 1.0000x reference)
import os
import numpy as np

os.environ.setdefault("TRN_ALLOW_ICI_ALL_TO_ALL", "1")

import concourse.bass as bass
import concourse.mybir as mybir
from concourse.bass_utils import run_bass_kernel_spmd

F32 = mybir.dt.float32
BF16 = mybir.dt.bfloat16
AF = mybir.ActivationFunctionType
ALU = mybir.AluOpType
AX = mybir.AxisListType.X

B, S, E, H, Dh = 2, 4096, 4096, 32, 128
NCORES = 8
NTOK = (B * S) // NCORES
NSUB = NTOK // 128
HL = H // NCORES
MASK_VAL = -66504.0
SM_SCALE = np.float32(1.0 / np.sqrt(Dh))
EXP_SHIFT = np.float32(-20.0)
LN_EPS = 1e-5
F1 = np.float32(1.0)
NQC = S // 512

TAPS = {}
PHASES = os.environ.get("MK_PHASES", "f")
Q1SUBS = int(os.environ.get("MK_Q1SUBS", "0"))


class Sem:
    def __init__(self, h):
        self.h = h
        self.n = 0

    def inc(self, inst, k=1):
        inst.then_inc(self.h, k)
        self.n += k


class Stream:

    def __init__(self, mk, name, sizes):
        self.sems = [mk(name + "_0"), mk(name + "_1")]
        self.cum = []
        tot = [0, 0]
        for i, sz in enumerate(sizes):
            tot[i % 2] += 16 * sz
            self.cum.append(tot[i % 2])
        self.totals = tot
        self.sizes = sizes
        self.gi = 0
        self.k = 0

    def dma(self, inst):
        inst.then_inc(self.sems[self.gi % 2].h, 16)
        self.sems[self.gi % 2].n += 16
        self.k += 1

    def end(self):
        assert self.k == self.sizes[self.gi], (self.gi, self.k, self.sizes[self.gi])
        self.gi += 1
        self.k = 0

    def group(self, inst):
        self.dma(inst); self.end()

    def wait_group(self, eng, i):
        eng.wait_ge(self.sems[i % 2].h, self.cum[i])

    def wait_all(self, eng):
        for p in range(2):
            if self.totals[p]:
                eng.wait_ge(self.sems[p].h, self.totals[p])


def build_program():
    nc = bass.Bass("TRN2", target_bir_lowering=False)

    din = lambda name, shape, dt=F32: nc.dram_tensor(name, list(shape), dt, kind="ExternalInput")
    x_tok = din("x_tok", [NTOK, E])
    qkv_w = din("qkv_w", [E, 3 * E])
    qkv_b = din("qkv_b", [128, 96])
    out_w = din("out_w", [E, E])
    out_b = din("out_b", [E])
    base_wT = din("base_wT", [E, E])
    base_b = din("base_b", [E])
    gate_wT = din("gate_wT", [E, 8])
    a_allT = din("a_allT", [E, 128])
    big_bT = din("big_bT", [128, E])
    cos2 = din("cos2", [128, NTOK])
    sin2 = din("sin2", [128, NTOK])
    ident_in = din("ident_in", [128, 128])
    trimask_in = din("trimask_in", [128, 4 * 512])
    ones_in = din("ones_in", [128, 128])
    rep_in = din("rep_in", [8, 128])

    y_out = nc.dram_tensor("y_out", [NTOK, E], F32, kind="ExternalOutput")
    taps = {tn: nc.dram_tensor(tn, list(tsh), tdt, kind="ExternalOutput")
            for tn, (tsh, tdt) in TAPS.items()}

    HSLAB = 128 * NTOK
    SH1QK = HL * 2 * HSLAB
    SH1V = HL * HSLAB
    a2a1qk_in = nc.dram_tensor("a2a1qk_in", [NCORES, SH1QK], BF16)
    a2a1qk_out = nc.dram_tensor("a2a1qk_out", [NCORES, SH1QK], BF16)
    a2a1v1_in = nc.dram_tensor("a2a1v1_in", [NCORES, 2 * HSLAB], BF16)
    a2a1v1_out = nc.dram_tensor("a2a1v1_out", [NCORES, 2 * HSLAB], BF16)
    a2a1v2_in = nc.dram_tensor("a2a1v2_in", [NCORES, 2 * HSLAB], BF16)
    a2a1v2_out = nc.dram_tensor("a2a1v2_out", [NCORES, 2 * HSLAB], BF16)
    a2a2a_in = nc.dram_tensor("a2a2a_in", [NCORES, 2 * HSLAB], BF16)
    a2a2a_out = nc.dram_tensor("a2a2a_out", [NCORES, 2 * HSLAB], BF16)
    a2a2b_in = nc.dram_tensor("a2a2b_in", [NCORES, 2 * HSLAB], BF16)
    a2a2b_out = nc.dram_tensor("a2a2b_out", [NCORES, 2 * HSLAB], BF16)
    y1_spill = nc.dram_tensor("y1_spill", [NTOK, E], BF16)
    ln2_spill = nc.dram_tensor("ln2_spill", [E, NTOK], BF16)

    def sh1(tens, i, hh, what):
        if what == "v":
            ap = tens.ap()[i:i + 1, hh * HSLAB:(hh + 1) * HSLAB]
            return ap.rearrange("o (t d) -> (o t) d", t=NTOK)
        off = {"q": 0, "k": HSLAB}[what]
        ap = tens.ap()[i:i + 1, hh * 2 * HSLAB + off: hh * 2 * HSLAB + off + HSLAB]
        return ap.rearrange("o (p t) -> (o p) t", p=128)

    def sh2(hh, i):
        def pick(tens):
            ap = tens.ap()[i:i + 1, (hh % 2) * HSLAB:(hh % 2 + 1) * HSLAB]
            return ap.rearrange("o (p t) -> (o p) t", p=128)
        return pick

    from contextlib import ExitStack
    es = ExitStack()
    with es:
        SB = lambda n, sh, dt=F32: es.enter_context(nc.sbuf_tensor(n, list(sh), dt))
        SEM = lambda n: Sem(es.enter_context(nc.semaphore(n)))

        ident = SB("ident", [128, 128], BF16)
        ones128 = SB("ones128", [128, 128], BF16)
        onesrow = SB("onesrow", [1, 128], BF16)
        trimask = SB("trimask", [128, 4 * 512], BF16)
        cos_s = SB("cos_s", [128, NTOK], BF16)
        sin_s = SB("sin_s", [128, NTOK], BF16)
        rep_s = SB("rep_s", [8, 128], BF16)
        qkvb_s = SB("qkvb_s", [128, 96], F32)
        outb_row = SB("outb_row", [1, E], BF16)
        baseb_row = SB("baseb_row", [1, E], BF16)
        stat6 = SB("stat6", [128, 8 * 6], F32)
        agg2 = SB("agg2", [128, 2], F32)
        aT_s = SB("aT_s", [128, 32 * 128], BF16)
        bB_s = SB("bB_s", [128, E], BF16)
        gwT_s = SB("gwT_s", [128, 32 * 8], BF16)
        wful_s = SB("wful_s", [128, 8], BF16)
        wfT_s = SB("wfT_s", [8, 128], BF16)
        wrep_s = SB("wrep_s", [128, NTOK], BF16)
        lowT_s = SB("lowT_s", [128, NTOK], BF16)
        lowW_s = SB("lowW_s", [128, NTOK], BF16)
        gl_s = SB("gl_s", [128, 8], F32)
        m1n_s = SB("m1n_s", [128, 1], F32)
        msk_s = SB("msk_s", [128, 8], F32)
        m2_s = SB("m2_s", [128, 1], F32)
        keep_s = SB("keep_s", [128, 8], F32)
        expl_s = SB("expl_s", [128, 8], F32)
        wsum_s = SB("wsum_s", [128, 1], F32)
        rws_s = SB("rws_s", [128, 1], F32)
        meanC = SB("meanC", [128, 16], F32)
        varC = SB("varC", [128, 16], F32)
        sqC = SB("sqC", [128, 16], F32)
        rstdC = SB("rstdC", [128, 16], F32)
        nmrC = SB("nmrC", [128, 16], F32)
        mean_s = SB("mean_s", [128, 1], F32)
        var_s = SB("var_s", [128, 1], F32)
        sqv_s = SB("sqv_s", [128, 1], F32)
        rstd_s = SB("rstd_s", [128, 1], F32)
        nmr_s = SB("nmr_s", [128, 1], F32)

        hv = SEM("hv"); hsc = SEM("hsc"); ht = SEM("ht")

        s_cast = SEM("s_cast")
        sq_tr = SEM("sq_tr"); sq_te = SEM("sq_te")
        sq_mm = SEM("sq_mm"); sq_e = SEM("sq_e")
        sq_r = SEM("sq_r"); sq_vt = SEM("sq_vt"); sq_ve = SEM("sq_ve")
        s_cc = SEM("s_cc")
        sa_sc = SEM("sa_sc"); sa_ex = SEM("sa_ex")
        sa_cp = SEM("sa_cp"); sa_rp = SEM("sa_rp")
        sa_pv = SEM("sa_pv"); sa_nm = SEM("sa_nm")
        so_mt = SEM("so_mt"); so_mm = SEM("so_mm"); so_ev = SEM("so_ev")
        so_y1 = SEM("so_y1")
        so_v1 = SEM("so_v1"); so_s1 = SEM("so_s1"); so_v2 = SEM("so_v2"); so_s2 = SEM("so_s2")
        so_v4 = SEM("so_v4"); so_s3 = SEM("so_s3"); so_v5 = SEM("so_v5"); so_s4 = SEM("so_s4"); so_tr = SEM("so_tr"); so_te = SEM("so_te")
        sf_g = SEM("sf_g"); sf_wv = SEM("sf_wv"); sf_low = SEM("sf_low")
        sf_mt = SEM("sf_mt"); sf_mm = SEM("sf_mm"); sf_ev = SEM("sf_ev")
        sf_o = SEM("sf_o")
        st_x = Stream(SEM, "st_x", [1] * NSUB)
        st_wq = Stream(SEM, "st_wq", [1] * 96)
        st_stg = Stream(SEM, "st_stg", [4] * 32)
        st_stgv = Stream(SEM, "st_stgv", [1] * 32)
        st_aqk = Stream(SEM, "st_aqk", [8] * 8)
        st_av = Stream(SEM, "st_av", [4] * 8)
        st_asta = Stream(SEM, "st_asta", [1] * 32)
        st_astb = Stream(SEM, "st_astb", [1] * 32)
        st_ocx = Stream(SEM, "st_ocx", [2] * 64)
        st_ow = Stream(SEM, "st_ow", [1] * 64)
        st_ox = Stream(SEM, "st_ox", [1] * NSUB)
        st_y1sp = Stream(SEM, "st_y1sp", [1] * NSUB)
        st_ln2sp = Stream(SEM, "st_ln2sp", [1] * NSUB)
        st_fl2 = Stream(SEM, "st_fl2", [1] * 8)
        st_fl2c = Stream(SEM, "st_fl2c", [1] * 64)
        st_fw = Stream(SEM, "st_fw", [1] * 64)
        st_fy1 = Stream(SEM, "st_fy1", [1] * NSUB)
        st_fst = Stream(SEM, "st_fst", [1] * NSUB)

        blk = es.enter_context(nc.Block())

        def ln_chain(src_ap, dst_ap, pre_apply_wait=None, nchunk=8):
            @blk.vector
            def _(v):
                v.drain()
                for ch in range(nchunk):
                    v.bn_stats(stat6[:, 6 * ch:6 * (ch + 1)], src_ap[:, 512 * ch:512 * (ch + 1)])
                v.drain()
                v.bn_aggr(agg2[:, :], stat6[:, 0:6 * nchunk])
                v.drain()
                v.tensor_copy(mean_s[:, :], agg2[:, 0:1])
                hv.inc(v.tensor_copy(var_s[:, :], agg2[:, 1:2]))

            @blk.scalar
            def _(s):
                s.wait_ge(hv.h, hv.n)
                hsc.inc(s.activation(sqv_s[:, :], var_s[:, :], AF.Sqrt,
                                     bias=np.float32(LN_EPS), scale=F1))

            @blk.vector
            def _(v):
                v.wait_ge(hsc.h, hsc.n)
                v.reciprocal(rstd_s[:, :], sqv_s[:, :])
                v.drain()
                hv.inc(v.scalar_tensor_tensor(nmr_s[:, :], mean_s[:, :], -1.0,
                                              rstd_s[:, :], ALU.mult, ALU.mult))

            @blk.scalar
            def _(s):
                s.wait_ge(hv.h, hv.n)
                if pre_apply_wait is not None:
                    pre_apply_wait(s)
                hsc.inc(s.activation(dst_ap, src_ap, AF.Identity,
                                     bias=nmr_s[:, :], scale=rstd_s[:, :]))

        @blk.gpsimd
        def _(g):
            for dst, src in [
                (ident[:, :], ident_in.ap()),
                (ones128[:, :], ones_in.ap()),
                (onesrow[:, :], ones_in.ap()[0:1, :]),
                (trimask[:, :], trimask_in.ap()),
                (cos_s[:, :], cos2.ap()),
                (sin_s[:, :], sin2.ap()),
                (rep_s[:, :], rep_in.ap()),
                (qkvb_s[:, :], qkv_b.ap()),
                (outb_row[:, :], out_b.ap()[None, :]),
                (baseb_row[:, :], base_b.ap()[None, :]),
                (aT_s[:, :].rearrange("p (c m) -> p c m", c=32),
                 a_allT.ap().rearrange("(c p) m -> p c m", p=128)),
                (bB_s[:, :], big_bT.ap()),
                (gwT_s[:, :].rearrange("p (c m) -> p c m", c=32),
                 gate_wT.ap().rearrange("(c p) m -> p c m", p=128)),
            ]:
                s_cast.inc(g.dma_start(out=dst, in_=src), 16)
        N_CONST = s_cast.n
        if PHASES == "q0":
            @blk.sync
            def _(sy):
                sy.wait_ge(s_cast.h, N_CONST)
            return nc

        with (
            nc.sbuf_tensor("ln1fm", [128, 32 * NTOK], BF16) as ln1fm,
            nc.sbuf_tensor("xbuf", [128, 2 * E], F32) as xbuf,
            nc.sbuf_tensor("ln1tm", [128, E], BF16) as ln1tm,
            nc.sbuf_tensor("wqpan", [128, 2 * 32 * 128], BF16) as wqpan,
            nc.sbuf_tensor("fmev", [128, 4 * NTOK], BF16) as fmev,
            nc.sbuf_tensor("ropeb", [128, 4 * NTOK], BF16) as ropeb,
            nc.sbuf_tensor("rtmp", [128, NTOK], BF16) as rtmp,
            nc.sbuf_tensor("vtev", [128, 2 * NTOK], BF16) as vtev,
            nc.psum_tensor("pq", [128, 3 * 1024], F32) as pq,
            nc.psum_tensor("pvt0", [128, 128], BF16) as pvt0,
            nc.psum_tensor("pvt1", [128, 128], BF16) as pvt1,
        ):
            _nsub_q = Q1SUBS if (PHASES == "q1" and Q1SUBS) else NSUB
            for sub in range(_nsub_q):
                @blk.sync
                def _(sy, sub=sub):
                    if sub >= 2:
                        sy.wait_ge(hsc.h, 2 * (sub - 1))
                    st_x.group(sy.dma_start(out=xbuf[:, (sub % 2) * E:(sub % 2 + 1) * E],
                                            in_=x_tok.ap()[sub * 128:(sub + 1) * 128, :]))

                @blk.vector
                def _(v, sub=sub):
                    st_x.wait_group(v, sub)

                xs = xbuf[:, (sub % 2) * E:(sub % 2 + 1) * E]

                def _lnwait(s, sub=sub):
                    if sub >= 1:
                        s.wait_ge(sq_tr.h, 32 * sub)

                ln_chain(xs, ln1tm[:, :], pre_apply_wait=_lnwait)

                @blk.tensor
                def _(t, sub=sub):
                    if sub == 0:
                        t.wait_ge(s_cast.h, N_CONST)
                    t.wait_ge(hsc.h, hsc.n)
                    for f in range(32):
                        i = sub * 32 + f
                        if i >= 2:
                            t.wait_ge(sq_te.h, i - 1)
                        sq_tr.inc(t.transpose((pvt0 if i % 2 == 0 else pvt1)[:, :],
                                              ln1tm[:, f * 128:(f + 1) * 128], ident[:, :]))

                @blk.vector
                def _(v, sub=sub):
                    for f in range(32):
                        i = sub * 32 + f
                        v.wait_ge(sq_tr.h, i + 1)
                        sq_te.inc(v.tensor_copy(
                            ln1fm[:, f * NTOK + sub * 128:f * NTOK + (sub + 1) * 128],
                            (pvt0 if i % 2 == 0 else pvt1)[:, :]))

            if PHASES == "q1":
                @blk.sync
                def _(sy):
                    sy.wait_ge(sq_te.h, 32 * _nsub_q)
                return nc

            VORD = [vc for vc in range(32) if vc % 4 < 2] + \
                   [vc for vc in range(32) if vc % 4 >= 2]
            CORD = list(range(64)) + [64 + vc for vc in VORD]

            @blk.gpsimd
            def _(g):
                for e, c in enumerate(CORD):
                    if e >= 2:
                        g.wait_ge(sq_mm.h, e - 1)
                    sl = (e % 2) * 32 * 128
                    st_wq.group(g.dma_start(
                        out=wqpan[:, sl:sl + 32 * 128].rearrange("p (fc m) -> p fc m", fc=32),
                        in_=qkv_w.ap()[:, c * 128:(c + 1) * 128].rearrange("(fc p) m -> p fc m", p=128)))

            @blk.tensor
            def _(t):
                def v_transposes(t, vc):
                    t.wait_ge(sq_e.h, 64 + vc + 1)
                    c = 64 + vc
                    for j in range(8):
                        i = vc * 8 + j
                        if i >= 2:
                            t.wait_ge(sq_ve.h, i - 1)
                        sq_vt.inc(t.transpose(
                            (pvt0 if i % 2 == 0 else pvt1)[:, :],
                            fmev[:, (c % 4) * NTOK + j * 128:(c % 4) * NTOK + (j + 1) * 128],
                            ident[:, :]))

                t.wait_ge(sq_te.h, 256)
                for e in range(96):
                    st_wq.wait_group(t, e)
                    if e >= 3:
                        t.wait_ge(sq_e.h, e - 2)
                    sl = (e % 2) * 32 * 128
                    pb = (e % 3) * 1024
                    inst = None
                    for f in range(32):
                        for half in range(2):
                            inst = t.matmul(
                                pq[:, pb + half * 512:pb + (half + 1) * 512],
                                wqpan[:, sl + f * 128:sl + (f + 1) * 128],
                                ln1fm[:, f * NTOK + half * 512:f * NTOK + (half + 1) * 512],
                                start=(f == 0), stop=(f == 31))
                    sq_mm.inc(inst)
                    if PHASES != "q2" and e >= 65:
                        v_transposes(t, e - 65)
                if PHASES != "q2":
                    v_transposes(t, 31)

            @blk.scalar
            def _(s):
                for e, c in enumerate(CORD):
                    s.wait_ge(sq_mm.h, e + 1)
                    if e >= 4 and PHASES != "q2":
                        ee = e - 4
                        if ee < 64:
                            s.wait_ge(sq_r.h, ee // 2 + 1)
                        else:
                            s.wait_ge(sq_vt.h, (ee - 64 + 1) * 8)
                    pb = (e % 3) * 1024
                    sq_e.inc(s.activation(fmev[:, (e % 4) * NTOK:(e % 4 + 1) * NTOK],
                                          pq[:, pb:pb + 1024], AF.Identity,
                                          bias=qkvb_s[:, c:c + 1], scale=F1))

            @blk.vector
            def _(v):
                if PHASES == "q2":
                    return
                for p in range(32):
                    v.wait_ge(sq_e.h, 2 * p + 2)
                    if p >= 2:
                        st_stg.wait_group(v, p - 2)
                    lo = fmev[:, (2 * p % 4) * NTOK:(2 * p % 4 + 1) * NTOK]
                    hi = fmev[:, ((2 * p + 1) % 4) * NTOK:((2 * p + 1) % 4 + 1) * NTOK]
                    olo = ropeb[:, (2 * p % 4) * NTOK:(2 * p % 4 + 1) * NTOK]
                    ohi = ropeb[:, ((2 * p + 1) % 4) * NTOK:((2 * p + 1) % 4 + 1) * NTOK]
                    v.drain()
                    v.tensor_tensor(olo, lo, cos_s[:, :], ALU.mult)
                    v.tensor_tensor(rtmp[:, :], hi, sin_s[:, :], ALU.mult)
                    v.drain()
                    v.tensor_tensor(olo, olo, rtmp[:, :], ALU.subtract)
                    v.tensor_tensor(ohi, hi, cos_s[:, :], ALU.mult)
                    v.drain()
                    v.tensor_tensor(rtmp[:, :], lo, sin_s[:, :], ALU.mult)
                    v.drain()
                    sq_r.inc(v.tensor_tensor(ohi, ohi, rtmp[:, :], ALU.add))

                for vc in range(32):
                    for j in range(8):
                        i = vc * 8 + j
                        v.wait_ge(sq_vt.h, i + 1)
                        if vc >= 2 and j == 0:
                            st_stgv.wait_group(v, vc - 2)
                        sq_ve.inc(v.tensor_copy(
                            vtev[:, (vc % 2) * NTOK + j * 128:(vc % 2) * NTOK + (j + 1) * 128],
                            (pvt0 if i % 2 == 0 else pvt1)[:, :]))

            @blk.sync
            def _(sy):
                if PHASES == "q2":
                    sy.wait_ge(sq_e.h, 96)
                    return
                for p in range(32):
                    sy.wait_ge(sq_r.h, p + 1)
                    what = "q" if p < 16 else "k"
                    hp = p % 16
                    olo = ropeb[:, (2 * p % 4) * NTOK:(2 * p % 4 + 1) * NTOK]
                    ohi = ropeb[:, ((2 * p + 1) % 4) * NTOK:((2 * p + 1) % 4 + 1) * NTOK]
                    for hloc in range(2):
                        h = 2 * hp + hloc
                        dst = sh1(a2a1qk_in, h // HL, h % HL, what)
                        st_stg.dma(sy.dma_start(out=dst[0:64, :],
                                                in_=olo[hloc * 64:hloc * 64 + 64, :]))
                        st_stg.dma(sy.dma_start(out=dst[64:128, :],
                                                in_=ohi[hloc * 64:hloc * 64 + 64, :]))
                    st_stg.end()
                for ve, vc in enumerate(VORD):
                    sy.wait_ge(sq_ve.h, (ve + 1) * 8)
                    vt = a2a1v1_in if vc % 4 < 2 else a2a1v2_in
                    dst = sh1(vt, vc // HL, (vc % HL) % 2, "v")
                    st_stgv.group(sy.dma_start(
                        out=dst.rearrange("(c p) d -> p c d", p=128),
                        in_=vtev[:, (ve % 2) * NTOK:(ve % 2 + 1) * NTOK].rearrange(
                            "p (c d) -> p c d", c=8)))

        if PHASES == "q2":
            return nc

        @blk.gpsimd
        def _(g):
            if PHASES == "q3":
                st_stg.wait_all(g)
                st_stgv.wait_all(g)
                return
            st_stg.wait_all(g)
            s_cc.inc(g.collective_compute(
                "AllToAll", ALU.bypass, replica_groups=[list(range(NCORES))],
                ins=[a2a1qk_in.ap().opt()], outs=[a2a1qk_out.ap().opt()]))
            st_stgv.wait_group(g, 15)
            s_cc.inc(g.collective_compute(
                "AllToAll", ALU.bypass, replica_groups=[list(range(NCORES))],
                ins=[a2a1v1_in.ap().opt()], outs=[a2a1v1_out.ap().opt()]))
            st_stgv.wait_all(g)
            s_cc.inc(g.collective_compute(
                "AllToAll", ALU.bypass, replica_groups=[list(range(NCORES))],
                ins=[a2a1v2_in.ap().opt()], outs=[a2a1v2_out.ap().opt()]))

        if PHASES == "q3":
            return nc
        if PHASES == "q":
            @blk.sync
            def _(sy):
                sy.wait_ge(s_cc.h, 3)
            return nc

        PISEQ = [b * HL + hh for hh in range(HL) for b in range(B)]
        PAIRS = []
        for o in range(8):
            pi = PISEQ[o]
            for Q in range(NQC):
                npr = 2 * (Q + 1)
                for pr in range(npr):
                    PAIRS.append((o, pi, Q, pr, npr, o * 8 + Q))
        NPAIR = len(PAIRS)
        pairs_through_gq = {}
        cnt = 0
        for (o, pi, Q, pr, npr, gq) in PAIRS:
            cnt += 1
            pairs_through_gq[gq] = cnt
        pairs_through_o = {o: pairs_through_gq[o * 8 + 7] for o in range(8)}

        with (
            nc.sbuf_tensor("qh", [128, 2 * S], BF16) as qh,
            nc.sbuf_tensor("kh", [128, 2 * S], BF16) as kh,
            nc.sbuf_tensor("vh", [128, 2 * S], BF16) as vh,
            nc.sbuf_tensor("expS", [128, 2 * 1024], BF16) as expS,
            nc.sbuf_tensor("recip", [128, 2 * 512], F32) as recip,
            nc.sbuf_tensor("part2", [1, 4 * 512], BF16) as part2,
            nc.sbuf_tensor("ctxev", [128, 2 * 512], BF16) as ctxev,
            nc.psum_tensor("psc", [128, 2 * 1024], F32) as psc,
            nc.psum_tensor("pctx", [128, 2 * 512], F32) as pctx,
            nc.psum_tensor("psm", [128, 2 * 512], F32) as psm,
        ):
            def emit_load(o):
                @blk.sync
                def _(sy):
                    pi = PISEQ[o]
                    b, hh = divmod(pi, HL)
                    if o == 0:
                        sy.wait_ge(s_cc.h, 1)
                    if o >= 2:
                        sy.wait_ge(sa_pv.h, pairs_through_o[o - 2])
                    sl = (o % 2)
                    for i in range(4):
                        rank = 4 * b + i
                        st_aqk.dma(sy.dma_start(out=qh[:, sl * S + i * NTOK:sl * S + (i + 1) * NTOK],
                                                in_=sh1(a2a1qk_out, rank, hh, "q")))
                        st_aqk.dma(sy.dma_start(out=kh[:, sl * S + i * NTOK:sl * S + (i + 1) * NTOK],
                                                in_=sh1(a2a1qk_out, rank, hh, "k")))
                    st_aqk.end()
                    if o == 0:
                        sy.wait_ge(s_cc.h, 2)
                    if o == 4:
                        sy.wait_ge(s_cc.h, 3)
                    vt = a2a1v1_out if hh < 2 else a2a1v2_out
                    for i in range(4):
                        rank = 4 * b + i
                        st_av.dma(sy.dma_start(
                            out=vh[:, sl * S + i * NTOK:sl * S + (i + 1) * NTOK].rearrange(
                                "p (c d) -> p c d", c=8),
                            in_=sh1(vt, rank, hh % 2, "v").rearrange("(c p) d -> p c d", p=128)))
                    st_av.end()

            def emit_stage(gq):
                @blk.sync
                def _(sy):
                    (o, Q) = divmod(gq, 8)
                    pi = PISEQ[o]
                    b, hh = divmod(pi, HL)
                    sy.wait_ge(sa_nm.h, gq + 1)
                    tens = a2a2a_in if hh < 2 else a2a2b_in
                    dst = sh2(hh, 4 * b + Q // 2)(tens)[:, (Q % 2) * 512:(Q % 2) * 512 + 512]
                    stt = st_asta if gq < 32 else st_astb
                    stt.group(sy.dma_start(out=dst,
                                           in_=ctxev[:, (gq % 2) * 512:(gq % 2 + 1) * 512]))

            emit_load(0)
            emit_load(1)
            for o in range(8):
                for Q in range(NQC):
                    emit_stage(o * 8 + Q)
                if o + 2 < 8:
                    emit_load(o + 2)

            @blk.tensor
            def _(t):
                def emit_scores(idx):
                    (o, pi, Q, pr, npr, gq) = PAIRS[idx]
                    sl = o % 2
                    if idx == 0 or PAIRS[idx - 1][0] != o:
                        st_aqk.wait_group(t, o)
                    if idx >= 2:
                        t.wait_ge(sa_ex.h, idx - 1)
                    diag = (pr >= npr - 2)
                    inst = None
                    for half in range(2):
                        kb = 2 * pr + half
                        dst = psc[:, (idx % 2) * 1024 + half * 512:(idx % 2) * 1024 + (half + 1) * 512]
                        inst = t.matmul(dst, kh[:, sl * S + kb * 128:sl * S + (kb + 1) * 128],
                                        qh[:, sl * S + Q * 512:sl * S + (Q + 1) * 512],
                                        start=True, stop=not diag)
                        if diag:
                            ti = kb - 4 * Q
                            inst = t.matmul(dst, ident[:, :], trimask[:, ti * 512:(ti + 1) * 512],
                                            start=False, stop=True)
                    sa_sc.inc(inst)

                def emit_pv(idx):
                    (o, pi, Q, pr, npr, gq) = PAIRS[idx]
                    sl = o % 2
                    t.wait_ge(sa_ex.h, idx + 1)
                    if idx == 0 or PAIRS[idx - 1][0] != o:
                        st_av.wait_group(t, o)
                    if pr == 0 and gq >= 2:
                        t.wait_ge(sa_nm.h, gq - 1)
                    inst = None
                    for half in range(2):
                        kb = 2 * pr + half
                        first = (pr == 0 and half == 0)
                        last = (pr == npr - 1 and half == 1)
                        ex = expS[:, (idx % 2) * 1024 + half * 512:(idx % 2) * 1024 + (half + 1) * 512]
                        t.matmul(pctx[:, (gq % 2) * 512:(gq % 2 + 1) * 512],
                                 vh[:, sl * S + kb * 128:sl * S + (kb + 1) * 128],
                                 ex, start=first, stop=last)
                        inst = t.matmul(psm[:, (gq % 2) * 512:(gq % 2 + 1) * 512],
                                        ones128[:, :], ex, start=first, stop=last)
                    sa_pv.inc(inst)

                for idx in range(NPAIR):
                    emit_scores(idx)
                    if idx >= 1:
                        emit_pv(idx - 1)
                emit_pv(NPAIR - 1)

            @blk.scalar
            def _(s):
                for idx in range(NPAIR):
                    s.wait_ge(sa_sc.h, idx + 1)
                    if idx == 0:
                        st_aqk.wait_group(s, 0)
                    if idx >= 2:
                        s.wait_ge(sa_pv.h, idx - 1)
                    sa_ex.inc(s.activation(expS[:, (idx % 2) * 1024:(idx % 2 + 1) * 1024],
                                           psc[:, (idx % 2) * 1024:(idx % 2 + 1) * 1024],
                                           AF.Exp, bias=EXP_SHIFT, scale=SM_SCALE))

            @blk.vector
            def _(v):
                for gq in range(64):
                    v.wait_ge(sa_pv.h, pairs_through_gq[gq])
                    if gq >= 2:
                        (st_asta if gq - 2 < 32 else st_astb).wait_group(v, (gq - 2) % 32)
                    v.drain()
                    v.reciprocal(recip[:, (gq % 2) * 512:(gq % 2 + 1) * 512],
                                 psm[:, (gq % 2) * 512:(gq % 2 + 1) * 512])
                    v.drain()
                    sa_nm.inc(v.tensor_tensor(ctxev[:, (gq % 2) * 512:(gq % 2 + 1) * 512],
                                              pctx[:, (gq % 2) * 512:(gq % 2 + 1) * 512],
                                              recip[:, (gq % 2) * 512:(gq % 2 + 1) * 512],
                                              ALU.mult))


        @blk.gpsimd
        def _(g):
            st_asta.wait_all(g)
            s_cc.inc(g.collective_compute(
                "AllToAll", ALU.bypass, replica_groups=[list(range(NCORES))],
                ins=[a2a2a_in.ap().opt()], outs=[a2a2a_out.ap().opt()]))
            st_astb.wait_all(g)
            s_cc.inc(g.collective_compute(
                "AllToAll", ALU.bypass, replica_groups=[list(range(NCORES))],
                ins=[a2a2b_in.ap().opt()], outs=[a2a2b_out.ap().opt()]))

        if PHASES == "a":
            @blk.sync
            def _(sy):
                sy.wait_ge(s_cc.h, 5)
            return nc

        with (
            nc.sbuf_tensor("attn_buf", [128, NSUB * E], BF16) as attn_buf,
            nc.sbuf_tensor("cxbuf", [128, 2 * 4096], BF16) as cxbuf,
            nc.sbuf_tensor("owbuf", [128, 2 * 2048], BF16) as owbuf,
            nc.sbuf_tensor("xbuf2", [128, 2 * E], BF16) as xbuf2,
            nc.sbuf_tensor("lnatmp", [128, E], BF16) as lnatmp,
            nc.sbuf_tensor("y1buf", [128, 2 * E], BF16) as y1buf,
            nc.sbuf_tensor("ln2tm", [128, E], BF16) as ln2tm,
            nc.sbuf_tensor("ln2sub", [128, 2 * 32 * 128], BF16) as ln2sub,
        ):
            with nc.psum_tensor("po", [128, 8 * 512], F32) as po:
                @blk.sync
                def _(sy):
                    sy.wait_ge(s_cc.h, 5)
                    for oc in range(8):
                        for fg in range(8):
                            i = oc * 8 + fg
                            if i >= 2:
                                sy.wait_ge(so_mt.h, i - 1)
                            base = (i % 2) * 4096
                            st_ocx.dma(sy.dma_start(
                                out=cxbuf[:, base:base + 2048].rearrange("p (hh t) -> p hh t", hh=2),
                                in_=a2a2a_out.ap()[fg:fg + 1, :].rearrange("o (hh p t) -> (o p) hh t", hh=2, p=128)))
                            st_ocx.dma(sy.dma_start(
                                out=cxbuf[:, base + 2048:base + 4096].rearrange("p (hh t) -> p hh t", hh=2),
                                in_=a2a2b_out.ap()[fg:fg + 1, :].rearrange("o (hh p t) -> (o p) hh t", hh=2, p=128)))
                            st_ocx.end()

                @blk.gpsimd
                def _(g):
                    g.wait_ge(s_cc.h, 5)
                    for oc in range(8):
                        for fg in range(8):
                            i = oc * 8 + fg
                            if i >= 2:
                                g.wait_ge(so_mt.h, i - 1)
                            st_ow.group(g.dma_start(
                                out=owbuf[:, (i % 2) * 2048:(i % 2 + 1) * 2048].rearrange(
                                    "p (fc c) -> p fc c", fc=4),
                                in_=out_w.ap()[fg * 512:(fg + 1) * 512, oc * 512:(oc + 1) * 512].rearrange(
                                    "(fc p) c -> p fc c", p=128)))

                @blk.tensor
                def _(t):
                    for oc in range(8):
                        if oc >= 1:
                            t.wait_ge(so_ev.h, oc)
                        inst = None
                        for fg in range(8):
                            i = oc * 8 + fg
                            st_ocx.wait_group(t, i)
                            st_ow.wait_group(t, i)
                            cb = (i % 2) * 4096
                            wb = (i % 2) * 2048
                            for fsub in range(4):
                                for sub in range(NSUB):
                                    inst = t.matmul(
                                        po[:, sub * 512:(sub + 1) * 512],
                                        cxbuf[:, cb + fsub * 1024 + sub * 128:cb + fsub * 1024 + (sub + 1) * 128],
                                        owbuf[:, wb + fsub * 512:wb + (fsub + 1) * 512],
                                        start=(fg == 0 and fsub == 0), stop=False)
                            so_mt.inc(inst)
                        for sub in range(NSUB):
                            inst = t.matmul(po[:, sub * 512:(sub + 1) * 512],
                                            onesrow[:, :], outb_row[0:1, oc * 512:(oc + 1) * 512],
                                            start=False, stop=True)
                        so_mm.inc(inst)

                @blk.scalar
                def _(s):
                    for oc in range(8):
                        s.wait_ge(so_mm.h, oc + 1)
                        dst = attn_buf[:, :].rearrange("p (sb c) -> p sb c", sb=NSUB)[:, :, oc * 512:(oc + 1) * 512]
                        src = po[:, :].rearrange("p (sb c) -> p sb c", sb=NSUB)
                        so_ev.inc(s.activation(dst, src, AF.Identity,
                                               bias=np.float32(0.0), scale=F1))

            with (nc.psum_tensor("ptro0", [128, 128], BF16) as ptro0,
                  nc.psum_tensor("ptro1", [128, 128], BF16) as ptro1,
                  nc.psum_tensor("pg", [128, 64], F32) as pg,
                  nc.psum_tensor("plow", [128, 1024], F32) as plow,
                  nc.psum_tensor("pwt", [8, 128], BF16) as pwt,
                  nc.psum_tensor("pwrep", [128, 128], F32) as pwrep):

                def emit_x(s):
                    @blk.gpsimd
                    def _(g):
                        if s >= 2:
                            g.wait_ge(so_y1.h, s - 1)
                        st_ox.group(g.dma_start(out=xbuf2[:, (s % 2) * E:(s % 2 + 1) * E],
                                                in_=x_tok.ap()[s * 128:(s + 1) * 128, :]))

                def emit_spill(s):
                    @blk.sync
                    def _(sy):
                        sy.wait_ge(so_y1.h, s + 1)
                        st_y1sp.group(sy.dma_start(
                            out=y1_spill.ap()[s * 128:(s + 1) * 128, :],
                            in_=y1buf[:, (s % 2) * E:(s % 2 + 1) * E]))
                        sy.wait_ge(so_te.h, 32 * (s + 1))
                        st_ln2sp.group(sy.dma_start(
                            out=ln2_spill.ap().rearrange("(fc p) t -> p fc t", p=128)[:, :, s * 128:(s + 1) * 128],
                            in_=ln2sub[:, (s % 2) * E:(s % 2 + 1) * E].rearrange("p (fc c) -> p fc c", fc=32)))

                for s in range(NSUB):
                    emit_spill(s)
                for s in range(NSUB):
                    emit_x(s)

                def V1(s):
                    @blk.vector
                    def _(v):
                        if s == 0:
                            v.wait_ge(so_ev.h, 8)
                        v.drain()
                        ab = attn_buf[:, s * E:(s + 1) * E]
                        for ch in range(8):
                            v.bn_stats(stat6[:, 6 * ch:6 * ch + 6], ab[:, 512 * ch:512 * ch + 512])
                        v.drain()
                        v.bn_aggr(agg2[:, :], stat6[:, :])
                        v.drain()
                        v.tensor_copy(meanC[:, s:s + 1], agg2[:, 0:1])
                        so_v1.inc(v.tensor_copy(varC[:, s:s + 1], agg2[:, 1:2]))

                def S1(s):
                    @blk.scalar
                    def _(s_):
                        s_.wait_ge(so_v1.h, s + 1)
                        so_s1.inc(s_.activation(sqC[:, s:s + 1], varC[:, s:s + 1], AF.Sqrt,
                                                bias=np.float32(LN_EPS), scale=F1))

                def V2(s):
                    @blk.vector
                    def _(v):
                        v.wait_ge(so_s1.h, s + 1)
                        v.drain()
                        v.reciprocal(rstdC[:, s:s + 1], sqC[:, s:s + 1])
                        v.drain()
                        so_v2.inc(v.scalar_tensor_tensor(nmrC[:, s:s + 1], meanC[:, s:s + 1],
                                                         -1.0, rstdC[:, s:s + 1], ALU.mult, ALU.mult))

                def S2(s):
                    @blk.scalar
                    def _(s_):
                        s_.wait_ge(so_v2.h, s + 1)
                        if s >= 1:
                            s_.wait_ge(so_y1.h, s)
                        so_s2.inc(s_.activation(lnatmp[:, :], attn_buf[:, s * E:(s + 1) * E],
                                                AF.Identity, bias=nmrC[:, s:s + 1],
                                                scale=rstdC[:, s:s + 1]))

                def V34(s):
                    @blk.vector
                    def _(v):
                        v.wait_ge(so_s2.h, s + 1)
                        st_ox.wait_group(v, s)
                        if s >= 2:
                            st_y1sp.wait_group(v, s - 2)
                            v.wait_ge(so_s4.h, s - 1)
                        v.drain()
                        so_y1.inc(v.tensor_tensor(y1buf[:, (s % 2) * E:(s % 2 + 1) * E],
                                                  lnatmp[:, :],
                                                  xbuf2[:, (s % 2) * E:(s % 2 + 1) * E],
                                                  ALU.add))
                        v.drain()
                        yb = y1buf[:, (s % 2) * E:(s % 2 + 1) * E]
                        for ch in range(8):
                            v.bn_stats(stat6[:, 6 * ch:6 * ch + 6], yb[:, 512 * ch:512 * ch + 512])
                        v.drain()
                        v.bn_aggr(agg2[:, :], stat6[:, :])
                        v.drain()
                        v.tensor_copy(meanC[:, 8 + s:9 + s], agg2[:, 0:1])
                        so_v4.inc(v.tensor_copy(varC[:, 8 + s:9 + s], agg2[:, 1:2]))

                def S3(s):
                    @blk.scalar
                    def _(s_):
                        s_.wait_ge(so_v4.h, s + 1)
                        so_s3.inc(s_.activation(sqC[:, 8 + s:9 + s], varC[:, 8 + s:9 + s], AF.Sqrt,
                                                bias=np.float32(LN_EPS), scale=F1))

                def V5(s):
                    @blk.vector
                    def _(v):
                        v.wait_ge(so_s3.h, s + 1)
                        v.drain()
                        v.reciprocal(rstdC[:, 8 + s:9 + s], sqC[:, 8 + s:9 + s])
                        v.drain()
                        so_v5.inc(v.scalar_tensor_tensor(nmrC[:, 8 + s:9 + s], meanC[:, 8 + s:9 + s],
                                                         -1.0, rstdC[:, 8 + s:9 + s], ALU.mult, ALU.mult))

                def S4(s):
                    @blk.scalar
                    def _(s_):
                        s_.wait_ge(so_v5.h, s + 1)
                        if s >= 1:
                            s_.wait_ge(so_tr.h, 32 * s)
                        so_s4.inc(s_.activation(ln2tm[:, :], y1buf[:, (s % 2) * E:(s % 2 + 1) * E],
                                                AF.Identity, bias=nmrC[:, 8 + s:9 + s],
                                                scale=rstdC[:, 8 + s:9 + s]))

                def T1G(s):
                    @blk.tensor
                    def _(t):
                        t.wait_ge(so_s4.h, s + 1)
                        for f in range(32):
                            i = s * 32 + f
                            if i >= 2:
                                t.wait_ge(so_te.h, i - 1)
                            so_tr.inc(t.transpose((ptro0 if i % 2 == 0 else ptro1)[:, :],
                                                  ln2tm[:, f * 128:(f + 1) * 128], ident[:, :]))
                        t.wait_ge(so_te.h, 32 * (s + 1))
                        l2s = ln2sub[:, (s % 2) * E:(s % 2 + 1) * E]
                        inst = None
                        for fc in range(32):
                            inst = t.matmul(pg[:, s * 8:(s + 1) * 8],
                                            l2s[:, fc * 128:(fc + 1) * 128],
                                            gwT_s[:, fc * 8:(fc + 1) * 8],
                                            start=(s == 0 and fc == 0),
                                            stop=(s == NSUB - 1 and fc == 31),
                                            skip_group_check=True)
                        for fc in range(32):
                            inst = t.matmul(plow[:, s * 128:(s + 1) * 128],
                                            aT_s[:, fc * 128:(fc + 1) * 128],
                                            l2s[:, fc * 128:(fc + 1) * 128],
                                            start=(s % 4 == 0 and fc == 0),
                                            stop=(s % 4 == 3 and fc == 31),
                                            skip_group_check=True)
                        sf_g.inc(inst)

                def V6(s):
                    @blk.vector
                    def _(v):
                        for f in range(32):
                            i = s * 32 + f
                            v.wait_ge(so_tr.h, i + 1)
                            if f == 0 and s >= 2:
                                st_ln2sp.wait_group(v, s - 2)
                                v.wait_ge(sf_g.h, s - 1)
                            so_te.inc(v.tensor_copy(
                                ln2sub[:, (s % 2) * E + f * 128:(s % 2) * E + (f + 1) * 128],
                                (ptro0 if i % 2 == 0 else ptro1)[:, :]))

                for tau in range(NSUB + 5):
                    if 0 <= tau - 4 < NSUB:
                        S4(tau - 4)
                        T1G(tau - 4)
                    if 0 <= tau - 3 < NSUB:
                        S3(tau - 3)
                    if 0 <= tau - 2 < NSUB:
                        S2(tau - 2)
                    if tau < NSUB:
                        V1(tau)
                        S1(tau)
                    if 0 <= tau - 1 < NSUB:
                        V2(tau - 1)
                    if 0 <= tau - 2 < NSUB:
                        V34(tau - 2)
                    if 0 <= tau - 3 < NSUB:
                        V5(tau - 3)
                        V6(tau - 3)

                for s in range(NSUB):
                    @blk.vector
                    def _(v, s=s):
                        v.wait_ge(sf_g.h, NSUB)
                        if s >= 1:
                            v.wait_ge(sf_wv.h, s)
                        v.drain()
                        v.tensor_copy(gl_s[:, :], pg[:, s * 8:(s + 1) * 8])
                        v.drain()
                        v.reduce_max(m1n_s[:, :], gl_s[:, :], axis=AX, negate=True)
                        v.drain()
                        v.tensor_scalar(msk_s[:, :], gl_s[:, :], m1n_s[:, :], None, ALU.add)
                        v.drain()
                        v.tensor_scalar(keep_s[:, :], msk_s[:, :], 0.0, None, ALU.is_ge)
                        v.drain()
                        v.scalar_tensor_tensor(msk_s[:, :], keep_s[:, :], -1e30,
                                               msk_s[:, :], ALU.mult, ALU.add)
                        v.drain()
                        hv.inc(v.reduce_max(m2_s[:, :], msk_s[:, :], axis=AX))

                    @blk.scalar
                    def _(s_, s=s):
                        s_.wait_ge(hv.h, hv.n)
                        hsc.inc(s_.activation(expl_s[:, :], gl_s[:, :], AF.Exp,
                                              bias=m1n_s[:, :], scale=F1))

                    @blk.vector
                    def _(v, s=s):
                        v.wait_ge(hsc.h, hsc.n)
                        v.drain()
                        v.tensor_scalar(msk_s[:, :], gl_s[:, :], m1n_s[:, :], None, ALU.add)
                        v.drain()
                        v.tensor_scalar(keep_s[:, :], msk_s[:, :], m2_s[:, :], None, ALU.is_ge)
                        v.drain()
                        v.tensor_tensor(expl_s[:, :], expl_s[:, :], keep_s[:, :], ALU.mult)
                        v.drain()
                        v.reduce_sum(wsum_s[:, :], expl_s[:, :], axis=AX)
                        v.drain()
                        v.reciprocal(rws_s[:, :], wsum_s[:, :])
                        v.drain()
                        hv.inc(v.tensor_scalar(wful_s[:, :], expl_s[:, :], rws_s[:, :],
                                               None, ALU.mult))

                    @blk.tensor
                    def _(t, s=s):
                        t.wait_ge(hv.h, hv.n)
                        ht.inc(t.transpose(pwt[:, :], wful_s[:, :], ident[:, :]))

                    @blk.vector
                    def _(v, s=s):
                        v.wait_ge(ht.h, ht.n)
                        hv.inc(v.tensor_copy(wfT_s[:, :], pwt[:, :]))

                    @blk.tensor
                    def _(t, s=s):
                        t.wait_ge(hv.h, hv.n)
                        ht.inc(t.matmul(pwrep[:, :], rep_s[:, :], wfT_s[:, :],
                                        start=True, stop=True))

                    @blk.vector
                    def _(v, s=s):
                        v.wait_ge(ht.h, ht.n)
                        sf_wv.inc(v.tensor_copy(wrep_s[:, s * 128:(s + 1) * 128], pwrep[:, :]))

                @blk.scalar
                def _(s_):
                    s_.wait_ge(sf_g.h, NSUB)
                    hsc.inc(s_.activation(lowT_s[:, :], plow[:, :], AF.Identity,
                                          bias=np.float32(0.0), scale=F1))
                HSC_LOWT = hsc.n

                @blk.vector
                def _(v):
                    v.wait_ge(sf_wv.h, NSUB)
                    v.wait_ge(hsc.h, HSC_LOWT)
                    v.drain()
                    sf_low.inc(v.tensor_tensor(lowW_s[:, :], lowT_s[:, :], wrep_s[:, :],
                                               ALU.mult))

        if PHASES == "o":
            @blk.sync
            def _(sy):
                st_y1sp.wait_all(sy)
                st_ln2sp.wait_all(sy)
            return nc

        fes = ExitStack()
        with fes:
            FSB = lambda n, sh, dt: fes.enter_context(nc.sbuf_tensor(n, list(sh), dt))
            zbuf = FSB("zbuf", [128, NSUB * E], BF16)
            l2buf = FSB("l2buf", [128, 2 * 4096], BF16)
            bwbuf = FSB("bwbuf", [128, 2 * 2048], BF16)
            zsum_s = FSB("zsum_s", [128, 64], F32)
            zsq_s = FSB("zsq_s", [128, 64], F32)
            zdump = FSB("zdump", [128, 512], BF16)
            y1ld = FSB("y1ld", [128, 2 * E], BF16)
            lnztmp = FSB("lnztmp", [128, E], F32)
            outbuf = FSB("outbuf", [128, 2 * E], F32)
            zr_s = FSB("zr_s", [128, 2], F32)
            with nc.psum_tensor("pz", [128, 8 * 512], F32) as pz:
                @blk.sync
                def _(sy):
                    st_y1sp.wait_all(sy)
                    st_ln2sp.wait_all(sy)
                    for oc in range(8):
                        for fg in range(8):
                            i = oc * 8 + fg
                            if i >= 2:
                                sy.wait_ge(sf_mt.h, i - 1)
                            st_fl2c.group(sy.dma_start(
                                out=l2buf[:, (i % 2) * 4096:(i % 2 + 1) * 4096].rearrange(
                                    "p (fc t) -> p fc t", fc=4),
                                in_=ln2_spill.ap()[fg * 512:(fg + 1) * 512, :].rearrange(
                                    "(fc p) t -> p fc t", p=128)))

                @blk.gpsimd
                def _(g):
                    st_y1sp.wait_all(g)
                    st_ln2sp.wait_all(g)
                    for oc in range(8):
                        for fg in range(8):
                            i = oc * 8 + fg
                            if i >= 2:
                                g.wait_ge(sf_mt.h, i - 1)
                            st_fw.group(g.dma_start(
                                out=bwbuf[:, (i % 2) * 2048:(i % 2 + 1) * 2048].rearrange(
                                    "p (fc c) -> p fc c", fc=4),
                                in_=base_wT.ap()[fg * 512:(fg + 1) * 512, oc * 512:(oc + 1) * 512].rearrange(
                                    "(fc p) c -> p fc c", p=128)))

                @blk.tensor
                def _(t):
                    t.wait_ge(sf_low.h, 1)
                    for oc in range(8):
                        if oc >= 1:
                            t.wait_ge(sf_ev.h, oc)
                        inst = None
                        for fg in range(8):
                            i = oc * 8 + fg
                            st_fl2c.wait_group(t, i)
                            st_fw.wait_group(t, i)
                            lb = (i % 2) * 4096
                            wb = (i % 2) * 2048
                            for fsub in range(4):
                                for sub in range(NSUB):
                                    inst = t.matmul(
                                        pz[:, sub * 512:(sub + 1) * 512],
                                        l2buf[:, lb + fsub * 1024 + sub * 128:lb + fsub * 1024 + (sub + 1) * 128],
                                        bwbuf[:, wb + fsub * 512:wb + (fsub + 1) * 512],
                                        start=(fg == 0 and fsub == 0), stop=False)
                            sf_mt.inc(inst)
                        for sub in range(NSUB):
                            t.matmul(pz[:, sub * 512:(sub + 1) * 512],
                                     lowW_s[:, sub * 128:(sub + 1) * 128],
                                     bB_s[:, oc * 512:(oc + 1) * 512],
                                     start=False, stop=False)
                        for sub in range(NSUB):
                            inst = t.matmul(pz[:, sub * 512:(sub + 1) * 512],
                                            onesrow[:, :], baseb_row[0:1, oc * 512:(oc + 1) * 512],
                                            start=False, stop=True)
                        sf_mm.inc(inst)

                @blk.scalar
                def _(s):
                    for oc in range(8):
                        s.wait_ge(sf_mm.h, oc + 1)
                        inst = None
                        for sub in range(NSUB):
                            s.activation(zbuf[:, sub * E + oc * 512:sub * E + (oc + 1) * 512],
                                         pz[:, sub * 512:(sub + 1) * 512], AF.Identity,
                                         bias=np.float32(0.0), scale=F1,
                                         accum_out=zsum_s[:, sub * 8 + oc:sub * 8 + oc + 1])
                            s.drain()
                            inst = s.activation(zdump[:, :],
                                                pz[:, sub * 512:(sub + 1) * 512], AF.Square,
                                                bias=np.float32(0.0), scale=F1,
                                                accum_out=zsq_s[:, sub * 8 + oc:sub * 8 + oc + 1])
                        sf_ev.inc(inst)

            if True:
                for sub in range(NSUB):
                    @blk.sync
                    def _(sy, sub=sub):
                        if sub >= 2:
                            sy.wait_ge(sf_o.h, sub - 1)
                        st_fy1.group(sy.dma_start(out=y1ld[:, (sub % 2) * E:(sub % 2 + 1) * E],
                                                  in_=y1_spill.ap()[sub * 128:(sub + 1) * 128, :]))

                    @blk.vector
                    def _(v, sub=sub):
                        if sub == 0:
                            v.wait_ge(sf_ev.h, 8)
                        v.drain()
                        v.reduce_sum(zr_s[:, 0:1], zsum_s[:, sub * 8:(sub + 1) * 8], axis=AX)
                        v.reduce_sum(zr_s[:, 1:2], zsq_s[:, sub * 8:(sub + 1) * 8], axis=AX)
                        v.drain()
                        v.tensor_scalar(mean_s[:, :], zr_s[:, 0:1], float(1.0 / E), None, ALU.mult)
                        v.tensor_scalar(var_s[:, :], zr_s[:, 1:2], float(1.0 / E), None, ALU.mult)
                        v.drain()
                        v.tensor_tensor(zr_s[:, 0:1], mean_s[:, :], mean_s[:, :], ALU.mult)
                        v.drain()
                        hv.inc(v.tensor_tensor(var_s[:, :], var_s[:, :], zr_s[:, 0:1], ALU.subtract))

                    @blk.scalar
                    def _(s, sub=sub):
                        s.wait_ge(hv.h, hv.n)
                        hsc.inc(s.activation(sqv_s[:, :], var_s[:, :], AF.Sqrt,
                                             bias=np.float32(LN_EPS), scale=F1))

                    @blk.vector
                    def _(v, sub=sub):
                        v.wait_ge(hsc.h, hsc.n)
                        v.reciprocal(rstd_s[:, :], sqv_s[:, :])
                        v.drain()
                        hv.inc(v.scalar_tensor_tensor(nmr_s[:, :], mean_s[:, :], -1.0,
                                                      rstd_s[:, :], ALU.mult, ALU.mult))

                    @blk.scalar
                    def _(s, sub=sub):
                        s.wait_ge(hv.h, hv.n)
                        hsc.inc(s.activation(lnztmp[:, :], zbuf[:, sub * E:(sub + 1) * E],
                                             AF.Identity, bias=nmr_s[:, :], scale=rstd_s[:, :]))

                    @blk.vector
                    def _(v, sub=sub):
                        v.wait_ge(hsc.h, hsc.n)
                        st_fy1.wait_group(v, sub)
                        if sub >= 2:
                            st_fst.wait_group(v, sub - 2)
                        sf_o.inc(v.tensor_tensor(outbuf[:, (sub % 2) * E:(sub % 2 + 1) * E],
                                                 lnztmp[:, :],
                                                 y1ld[:, (sub % 2) * E:(sub % 2 + 1) * E],
                                                 ALU.add))

                    @blk.sync
                    def _(sy, sub=sub):
                        sy.wait_ge(sf_o.h, sub + 1)
                        st_fst.group(sy.dma_start(out=y_out.ap()[sub * 128:(sub + 1) * 128, :],
                                                  in_=outbuf[:, (sub % 2) * E:(sub % 2 + 1) * E]))

                @blk.sync
                def _(sy):
                    st_fst.wait_all(sy)

    return nc


def _host_prep(L0_attn_hidden_states, L0_attn_qkv_w, L0_attn_qkv_b, L0_attn_out_w,
               L0_attn_out_b, L0_ffn_gate_w, L0_ffn_base_w, L0_ffn_base_b,
               L0_ffn_lora_A, L0_ffn_lora_B):
    x = np.ascontiguousarray(np.asarray(L0_attn_hidden_states, dtype=np.float32).reshape(B * S, E))
    qkv_w = np.asarray(L0_attn_qkv_w, dtype=np.float32)
    qkv_b = np.asarray(L0_attn_qkv_b, dtype=np.float32)

    perm = []
    for p in range(16):
        h0, h1 = 2 * p, 2 * p + 1
        perm += list(range(h0 * 128, h0 * 128 + 64)) + list(range(h1 * 128, h1 * 128 + 64))
        perm += list(range(h0 * 128 + 64, h0 * 128 + 128)) + list(range(h1 * 128 + 64, h1 * 128 + 128))
    perm = np.asarray(perm)
    qkv_w_p = np.empty_like(qkv_w)
    qkv_w_p[:, 0:E] = qkv_w[:, 0:E][:, perm]
    qkv_w_p[:, E:2 * E] = qkv_w[:, E:2 * E][:, perm]
    qkv_w_p[:, 2 * E:] = qkv_w[:, 2 * E:]
    qkv_b_p = np.empty_like(qkv_b)
    qkv_b_p[0:E] = qkv_b[0:E][perm]
    qkv_b_p[E:2 * E] = qkv_b[E:2 * E][perm]
    qkv_b_p[2 * E:] = qkv_b[2 * E:]
    qkv_b_p = np.ascontiguousarray(qkv_b_p.reshape(96, 128).T)

    out_w = np.ascontiguousarray(np.asarray(L0_attn_out_w, dtype=np.float32))
    out_b = np.asarray(L0_attn_out_b, dtype=np.float32)
    base_wT = np.ascontiguousarray(np.asarray(L0_ffn_base_w, dtype=np.float32).T)
    base_b = np.asarray(L0_ffn_base_b, dtype=np.float32)
    gate_wT = np.ascontiguousarray(np.asarray(L0_ffn_gate_w, dtype=np.float32).T)
    a_allT = np.ascontiguousarray(np.asarray(L0_ffn_lora_A, dtype=np.float32).transpose(2, 0, 1).reshape(E, 128))
    big_bT = np.ascontiguousarray(np.asarray(L0_ffn_lora_B, dtype=np.float32).transpose(0, 2, 1).reshape(128, E))

    ident = np.eye(128, dtype=np.float32)
    ones = np.ones((128, 128), dtype=np.float32)
    rep = np.zeros((8, 128), dtype=np.float32)
    for e in range(8):
        rep[e, e * 16:(e + 1) * 16] = 2.0
    trimask = np.zeros((128, 4 * 512), dtype=np.float32)
    for ti in range(4):
        kl = np.arange(128)[:, None]
        ql = np.arange(512)[None, :]
        trimask[:, ti * 512:(ti + 1) * 512] = np.where(128 * ti + kl > ql, np.float32(MASK_VAL), 0.0)

    inv_freq = 1.0 / (ROPE_BASE_F ** (np.arange(64, dtype=np.float64) / 64.0))
    in_maps = []
    shared = dict(qkv_w=qkv_w_p, qkv_b=qkv_b_p, out_w=out_w, out_b=out_b,
                  base_wT=base_wT, base_b=base_b, gate_wT=gate_wT,
                  a_allT=a_allT, big_bT=big_bT, ident_in=ident,
                  trimask_in=trimask, ones_in=ones, rep_in=rep)
    for c in range(NCORES):
        pos = (np.arange(NTOK, dtype=np.int64) + NTOK * c) % S
        ang = pos[:, None].astype(np.float64) * inv_freq[None, :]
        cosT = np.cos(ang).T.astype(np.float32)
        sinT = np.sin(ang).T.astype(np.float32)
        m = dict(shared)
        m["x_tok"] = np.ascontiguousarray(x[c * NTOK:(c + 1) * NTOK])
        m["cos2"] = np.ascontiguousarray(np.vstack([cosT, cosT]))
        m["sin2"] = np.ascontiguousarray(np.vstack([sinT, sinT]))
        in_maps.append(m)
    return in_maps


ROPE_BASE_F = 10000.0

_NC_CACHE = []


def kernel(**inputs):
    in_maps = _host_prep(**inputs)
    if not _NC_CACHE:
        _NC_CACHE.append(build_program())
    nc = _NC_CACHE[0]
    res = run_bass_kernel_spmd(nc, in_maps, core_ids=list(range(NCORES)))
    y = np.concatenate([np.asarray(r["y_out"]) for r in res.results], axis=0)
    return y.reshape(B, S, E)


# version 12
# speedup vs baseline: 1.0049x; 1.0049x over previous
import os
import numpy as np

os.environ.setdefault("TRN_ALLOW_ICI_ALL_TO_ALL", "1")

import concourse.bass as bass
import concourse.mybir as mybir
from concourse.bass_utils import run_bass_kernel_spmd

F32 = mybir.dt.float32
BF16 = mybir.dt.bfloat16
AF = mybir.ActivationFunctionType
ALU = mybir.AluOpType
AX = mybir.AxisListType.X

B, S, E, H, Dh = 2, 4096, 4096, 32, 128
NCORES = 8
NTOK = (B * S) // NCORES
NSUB = NTOK // 128
HL = H // NCORES
MASK_VAL = -66504.0
SM_SCALE = np.float32(1.0 / np.sqrt(Dh))
EXP_SHIFT = np.float32(-20.0)
LN_EPS = 1e-5
F1 = np.float32(1.0)
NQC = S // 512

TAPS = {}
PHASES = os.environ.get("MK_PHASES", "f")
Q1SUBS = int(os.environ.get("MK_Q1SUBS", "0"))


class Sem:
    def __init__(self, h):
        self.h = h
        self.n = 0

    def inc(self, inst, k=1):
        inst.then_inc(self.h, k)
        self.n += k


class Stream:

    def __init__(self, mk, name, sizes):
        self.sems = [mk(name + "_0"), mk(name + "_1")]
        self.cum = []
        tot = [0, 0]
        for i, sz in enumerate(sizes):
            tot[i % 2] += 16 * sz
            self.cum.append(tot[i % 2])
        self.totals = tot
        self.sizes = sizes
        self.gi = 0
        self.k = 0

    def dma(self, inst):
        inst.then_inc(self.sems[self.gi % 2].h, 16)
        self.sems[self.gi % 2].n += 16
        self.k += 1

    def end(self):
        assert self.k == self.sizes[self.gi], (self.gi, self.k, self.sizes[self.gi])
        self.gi += 1
        self.k = 0

    def group(self, inst):
        self.dma(inst); self.end()

    def wait_group(self, eng, i):
        eng.wait_ge(self.sems[i % 2].h, self.cum[i])

    def wait_all(self, eng):
        for p in range(2):
            if self.totals[p]:
                eng.wait_ge(self.sems[p].h, self.totals[p])


def build_program():
    nc = bass.Bass("TRN2", target_bir_lowering=False)

    din = lambda name, shape, dt=F32: nc.dram_tensor(name, list(shape), dt, kind="ExternalInput")
    x_tok = din("x_tok", [NTOK, E])
    qkv_w = din("qkv_w", [E, 3 * E])
    qkv_b = din("qkv_b", [128, 96])
    out_w = din("out_w", [E, E])
    out_b = din("out_b", [E])
    base_wT = din("base_wT", [E, E])
    base_b = din("base_b", [E])
    gate_wT = din("gate_wT", [E, 8])
    a_allT = din("a_allT", [E, 128])
    big_bT = din("big_bT", [128, E])
    cos2 = din("cos2", [128, NTOK])
    sin2 = din("sin2", [128, NTOK])
    ident_in = din("ident_in", [128, 128])
    trimask_in = din("trimask_in", [128, 4 * 512])
    ones_in = din("ones_in", [128, 128])
    rep_in = din("rep_in", [8, 128])

    y_out = nc.dram_tensor("y_out", [NTOK, E], F32, kind="ExternalOutput")
    taps = {tn: nc.dram_tensor(tn, list(tsh), tdt, kind="ExternalOutput")
            for tn, (tsh, tdt) in TAPS.items()}

    HSLAB = 128 * NTOK
    SH1QK = HL * 2 * HSLAB
    SH1V = HL * HSLAB
    a2a1qk_in = nc.dram_tensor("a2a1qk_in", [NCORES, SH1QK], BF16)
    a2a1qk_out = nc.dram_tensor("a2a1qk_out", [NCORES, SH1QK], BF16)
    a2a1v1_in = nc.dram_tensor("a2a1v1_in", [NCORES, 2 * HSLAB], BF16)
    a2a1v1_out = nc.dram_tensor("a2a1v1_out", [NCORES, 2 * HSLAB], BF16)
    a2a1v2_in = nc.dram_tensor("a2a1v2_in", [NCORES, 2 * HSLAB], BF16)
    a2a1v2_out = nc.dram_tensor("a2a1v2_out", [NCORES, 2 * HSLAB], BF16)
    a2a2a_in = nc.dram_tensor("a2a2a_in", [NCORES, 2 * HSLAB], BF16)
    a2a2a_out = nc.dram_tensor("a2a2a_out", [NCORES, 2 * HSLAB], BF16)
    a2a2b_in = nc.dram_tensor("a2a2b_in", [NCORES, 2 * HSLAB], BF16)
    a2a2b_out = nc.dram_tensor("a2a2b_out", [NCORES, 2 * HSLAB], BF16)
    y1_spill = nc.dram_tensor("y1_spill", [NTOK, E], BF16)
    ln2_spill = nc.dram_tensor("ln2_spill", [E, NTOK], BF16)

    def sh1(tens, i, hh, what):
        if what == "v":
            ap = tens.ap()[i:i + 1, hh * HSLAB:(hh + 1) * HSLAB]
            return ap.rearrange("o (t d) -> (o t) d", t=NTOK)
        off = {"q": 0, "k": HSLAB}[what]
        ap = tens.ap()[i:i + 1, hh * 2 * HSLAB + off: hh * 2 * HSLAB + off + HSLAB]
        return ap.rearrange("o (p t) -> (o p) t", p=128)

    def sh2(hh, i):
        def pick(tens):
            ap = tens.ap()[i:i + 1, (hh % 2) * HSLAB:(hh % 2 + 1) * HSLAB]
            return ap.rearrange("o (p t) -> (o p) t", p=128)
        return pick

    from contextlib import ExitStack
    es = ExitStack()
    with es:
        SB = lambda n, sh, dt=F32: es.enter_context(nc.sbuf_tensor(n, list(sh), dt))
        SEM = lambda n: Sem(es.enter_context(nc.semaphore(n)))

        ident = SB("ident", [128, 128], BF16)
        ones128 = SB("ones128", [128, 128], BF16)
        onesrow = SB("onesrow", [1, 128], BF16)
        trimask = SB("trimask", [128, 4 * 512], BF16)
        cos_s = SB("cos_s", [128, NTOK], BF16)
        sin_s = SB("sin_s", [128, NTOK], BF16)
        rep_s = SB("rep_s", [8, 128], BF16)
        qkvb_s = SB("qkvb_s", [128, 96], F32)
        outb_row = SB("outb_row", [1, E], BF16)
        baseb_row = SB("baseb_row", [1, E], BF16)
        stat6 = SB("stat6", [128, 8 * 6], F32)
        agg2 = SB("agg2", [128, 2], F32)
        aT_s = SB("aT_s", [128, 32 * 128], BF16)
        bB_s = SB("bB_s", [128, E], BF16)
        gwT_s = SB("gwT_s", [128, 32 * 8], BF16)
        wful_s = SB("wful_s", [128, 8], BF16)
        wfT_s = SB("wfT_s", [8, 128], BF16)
        wrep_s = SB("wrep_s", [128, NTOK], BF16)
        lowT_s = SB("lowT_s", [128, NTOK], BF16)
        lowW_s = SB("lowW_s", [128, NTOK], BF16)
        gl_s = SB("gl_s", [128, 8], F32)
        m1n_s = SB("m1n_s", [128, 1], F32)
        msk_s = SB("msk_s", [128, 8], F32)
        m2_s = SB("m2_s", [128, 1], F32)
        keep_s = SB("keep_s", [128, 8], F32)
        expl_s = SB("expl_s", [128, 8], F32)
        wsum_s = SB("wsum_s", [128, 1], F32)
        rws_s = SB("rws_s", [128, 1], F32)
        meanC = SB("meanC", [128, 16], F32)
        varC = SB("varC", [128, 16], F32)
        sqC = SB("sqC", [128, 16], F32)
        rstdC = SB("rstdC", [128, 16], F32)
        nmrC = SB("nmrC", [128, 16], F32)
        mean_s = SB("mean_s", [128, 1], F32)
        var_s = SB("var_s", [128, 1], F32)
        sqv_s = SB("sqv_s", [128, 1], F32)
        rstd_s = SB("rstd_s", [128, 1], F32)
        nmr_s = SB("nmr_s", [128, 1], F32)

        hv = SEM("hv"); hsc = SEM("hsc"); ht = SEM("ht")

        s_cast = SEM("s_cast")
        sq_tr = SEM("sq_tr"); sq_te = SEM("sq_te")
        sq_mm = SEM("sq_mm"); sq_e = SEM("sq_e")
        sq_r = SEM("sq_r"); sq_vt = SEM("sq_vt"); sq_ve = SEM("sq_ve")
        s_cc = SEM("s_cc")
        sa_sc = SEM("sa_sc"); sa_ex = SEM("sa_ex")
        sa_cp = SEM("sa_cp"); sa_rp = SEM("sa_rp")
        sa_pv = SEM("sa_pv"); sa_nm = SEM("sa_nm")
        so_mt = SEM("so_mt"); so_mm = SEM("so_mm"); so_ev = SEM("so_ev")
        so_y1 = SEM("so_y1")
        so_v1 = SEM("so_v1"); so_s1 = SEM("so_s1"); so_v2 = SEM("so_v2"); so_s2 = SEM("so_s2")
        so_v4 = SEM("so_v4"); so_s3 = SEM("so_s3"); so_v5 = SEM("so_v5"); so_s4 = SEM("so_s4"); so_tr = SEM("so_tr"); so_te = SEM("so_te")
        sf_g = SEM("sf_g"); sf_wv = SEM("sf_wv"); sf_low = SEM("sf_low")
        sf_mt = SEM("sf_mt"); sf_mm = SEM("sf_mm"); sf_ev = SEM("sf_ev")
        sf_o = SEM("sf_o")
        st_x = Stream(SEM, "st_x", [1] * NSUB)
        st_wq = Stream(SEM, "st_wq", [1] * 96)
        st_stg = Stream(SEM, "st_stg", [4] * 32)
        st_stgv = Stream(SEM, "st_stgv", [1] * 32)
        st_aqk = Stream(SEM, "st_aqk", [8] * 8)
        st_av = Stream(SEM, "st_av", [4] * 8)
        st_asta = Stream(SEM, "st_asta", [1] * 32)
        st_astb = Stream(SEM, "st_astb", [1] * 32)
        st_ocx = Stream(SEM, "st_ocx", [2] * 64)
        st_ow = Stream(SEM, "st_ow", [1] * 64)
        st_ox = Stream(SEM, "st_ox", [1] * NSUB)
        st_y1sp = Stream(SEM, "st_y1sp", [1] * NSUB)
        st_ln2sp = Stream(SEM, "st_ln2sp", [1] * NSUB)
        st_fl2 = Stream(SEM, "st_fl2", [1] * 8)
        st_fl2c = Stream(SEM, "st_fl2c", [1] * 64)
        st_fw = Stream(SEM, "st_fw", [1] * 64)
        st_fy1 = Stream(SEM, "st_fy1", [1] * NSUB)
        st_fst = Stream(SEM, "st_fst", [1] * NSUB)

        blk = es.enter_context(nc.Block())

        def ln_chain(src_ap, dst_ap, pre_apply_wait=None, nchunk=8):
            @blk.vector
            def _(v):
                v.drain()
                for ch in range(nchunk):
                    v.bn_stats(stat6[:, 6 * ch:6 * (ch + 1)], src_ap[:, 512 * ch:512 * (ch + 1)])
                v.drain()
                v.bn_aggr(agg2[:, :], stat6[:, 0:6 * nchunk])
                v.drain()
                v.tensor_copy(mean_s[:, :], agg2[:, 0:1])
                hv.inc(v.tensor_copy(var_s[:, :], agg2[:, 1:2]))

            @blk.scalar
            def _(s):
                s.wait_ge(hv.h, hv.n)
                hsc.inc(s.activation(sqv_s[:, :], var_s[:, :], AF.Sqrt,
                                     bias=np.float32(LN_EPS), scale=F1))

            @blk.vector
            def _(v):
                v.wait_ge(hsc.h, hsc.n)
                v.reciprocal(rstd_s[:, :], sqv_s[:, :])
                v.drain()
                hv.inc(v.scalar_tensor_tensor(nmr_s[:, :], mean_s[:, :], -1.0,
                                              rstd_s[:, :], ALU.mult, ALU.mult))

            @blk.scalar
            def _(s):
                s.wait_ge(hv.h, hv.n)
                if pre_apply_wait is not None:
                    pre_apply_wait(s)
                hsc.inc(s.activation(dst_ap, src_ap, AF.Identity,
                                     bias=nmr_s[:, :], scale=rstd_s[:, :]))

        @blk.gpsimd
        def _(g):
            for dst, src in [
                (ident[:, :], ident_in.ap()),
                (ones128[:, :], ones_in.ap()),
                (onesrow[:, :], ones_in.ap()[0:1, :]),
                (trimask[:, :], trimask_in.ap()),
                (cos_s[:, :], cos2.ap()),
                (sin_s[:, :], sin2.ap()),
                (rep_s[:, :], rep_in.ap()),
                (qkvb_s[:, :], qkv_b.ap()),
                (outb_row[:, :], out_b.ap()[None, :]),
                (baseb_row[:, :], base_b.ap()[None, :]),
                (aT_s[:, :].rearrange("p (c m) -> p c m", c=32),
                 a_allT.ap().rearrange("(c p) m -> p c m", p=128)),
                (bB_s[:, :], big_bT.ap()),
                (gwT_s[:, :].rearrange("p (c m) -> p c m", c=32),
                 gate_wT.ap().rearrange("(c p) m -> p c m", p=128)),
            ]:
                s_cast.inc(g.dma_start(out=dst, in_=src), 16)
        N_CONST = s_cast.n
        if PHASES == "q0":
            @blk.sync
            def _(sy):
                sy.wait_ge(s_cast.h, N_CONST)
            return nc

        with (
            nc.sbuf_tensor("ln1fm", [128, 32 * NTOK], BF16) as ln1fm,
            nc.sbuf_tensor("xbuf", [128, 2 * E], F32) as xbuf,
            nc.sbuf_tensor("ln1tm", [128, E], BF16) as ln1tm,
            nc.sbuf_tensor("wqpan", [128, 2 * 32 * 128], BF16) as wqpan,
            nc.sbuf_tensor("fmev", [128, 4 * NTOK], BF16) as fmev,
            nc.sbuf_tensor("ropeb", [128, 4 * NTOK], BF16) as ropeb,
            nc.sbuf_tensor("rtmp", [128, NTOK], BF16) as rtmp,
            nc.sbuf_tensor("vtev", [128, 2 * NTOK], BF16) as vtev,
            nc.psum_tensor("pq", [128, 3 * 1024], F32) as pq,
            nc.psum_tensor("pvt0", [128, 128], BF16) as pvt0,
            nc.psum_tensor("pvt1", [128, 128], BF16) as pvt1,
        ):
            _nsub_q = Q1SUBS if (PHASES == "q1" and Q1SUBS) else NSUB
            for sub in range(_nsub_q):
                @blk.sync
                def _(sy, sub=sub):
                    if sub >= 2:
                        sy.wait_ge(hsc.h, 2 * (sub - 1))
                    st_x.group(sy.dma_start(out=xbuf[:, (sub % 2) * E:(sub % 2 + 1) * E],
                                            in_=x_tok.ap()[sub * 128:(sub + 1) * 128, :]))

                @blk.vector
                def _(v, sub=sub):
                    st_x.wait_group(v, sub)

                xs = xbuf[:, (sub % 2) * E:(sub % 2 + 1) * E]

                def _lnwait(s, sub=sub):
                    if sub >= 1:
                        s.wait_ge(sq_tr.h, 32 * sub)

                ln_chain(xs, ln1tm[:, :], pre_apply_wait=_lnwait)

                @blk.tensor
                def _(t, sub=sub):
                    if sub == 0:
                        t.wait_ge(s_cast.h, N_CONST)
                    t.wait_ge(hsc.h, hsc.n)
                    for f in range(32):
                        i = sub * 32 + f
                        if i >= 2:
                            t.wait_ge(sq_te.h, i - 1)
                        sq_tr.inc(t.transpose((pvt0 if i % 2 == 0 else pvt1)[:, :],
                                              ln1tm[:, f * 128:(f + 1) * 128], ident[:, :]))

                @blk.vector
                def _(v, sub=sub):
                    for f in range(32):
                        i = sub * 32 + f
                        v.wait_ge(sq_tr.h, i + 1)
                        sq_te.inc(v.tensor_copy(
                            ln1fm[:, f * NTOK + sub * 128:f * NTOK + (sub + 1) * 128],
                            (pvt0 if i % 2 == 0 else pvt1)[:, :]))

            if PHASES == "q1":
                @blk.sync
                def _(sy):
                    sy.wait_ge(sq_te.h, 32 * _nsub_q)
                return nc

            VORD = [vc for vc in range(32) if vc % 4 < 2] + \
                   [vc for vc in range(32) if vc % 4 >= 2]
            CORD = list(range(64)) + [64 + vc for vc in VORD]

            @blk.gpsimd
            def _(g):
                for e, c in enumerate(CORD):
                    if e >= 2:
                        g.wait_ge(sq_mm.h, e - 1)
                    sl = (e % 2) * 32 * 128
                    st_wq.group(g.dma_start(
                        out=wqpan[:, sl:sl + 32 * 128].rearrange("p (fc m) -> p fc m", fc=32),
                        in_=qkv_w.ap()[:, c * 128:(c + 1) * 128].rearrange("(fc p) m -> p fc m", p=128)))

            @blk.tensor
            def _(t):
                def v_transposes(t, vc):
                    t.wait_ge(sq_e.h, 64 + vc + 1)
                    c = 64 + vc
                    for j in range(8):
                        i = vc * 8 + j
                        if i >= 2:
                            t.wait_ge(sq_ve.h, i - 1)
                        sq_vt.inc(t.transpose(
                            (pvt0 if i % 2 == 0 else pvt1)[:, :],
                            fmev[:, (c % 4) * NTOK + j * 128:(c % 4) * NTOK + (j + 1) * 128],
                            ident[:, :]))

                t.wait_ge(sq_te.h, 256)
                for e in range(96):
                    st_wq.wait_group(t, e)
                    if e >= 3:
                        t.wait_ge(sq_e.h, e - 2)
                    sl = (e % 2) * 32 * 128
                    pb = (e % 3) * 1024
                    inst = None
                    for f in range(32):
                        for half in range(2):
                            inst = t.matmul(
                                pq[:, pb + half * 512:pb + (half + 1) * 512],
                                wqpan[:, sl + f * 128:sl + (f + 1) * 128],
                                ln1fm[:, f * NTOK + half * 512:f * NTOK + (half + 1) * 512],
                                start=(f == 0), stop=(f == 31))
                    sq_mm.inc(inst)
                    if PHASES != "q2" and e >= 65:
                        v_transposes(t, e - 65)
                if PHASES != "q2":
                    v_transposes(t, 31)

            @blk.scalar
            def _(s):
                for e, c in enumerate(CORD):
                    s.wait_ge(sq_mm.h, e + 1)
                    if e >= 4 and PHASES != "q2":
                        ee = e - 4
                        if ee < 64:
                            s.wait_ge(sq_r.h, ee // 2 + 1)
                        else:
                            s.wait_ge(sq_vt.h, (ee - 64 + 1) * 8)
                    pb = (e % 3) * 1024
                    sq_e.inc(s.activation(fmev[:, (e % 4) * NTOK:(e % 4 + 1) * NTOK],
                                          pq[:, pb:pb + 1024], AF.Identity,
                                          bias=qkvb_s[:, c:c + 1], scale=F1))

            @blk.vector
            def _(v):
                if PHASES == "q2":
                    return
                for p in range(32):
                    v.wait_ge(sq_e.h, 2 * p + 2)
                    if p >= 2:
                        st_stg.wait_group(v, p - 2)
                    lo = fmev[:, (2 * p % 4) * NTOK:(2 * p % 4 + 1) * NTOK]
                    hi = fmev[:, ((2 * p + 1) % 4) * NTOK:((2 * p + 1) % 4 + 1) * NTOK]
                    olo = ropeb[:, (2 * p % 4) * NTOK:(2 * p % 4 + 1) * NTOK]
                    ohi = ropeb[:, ((2 * p + 1) % 4) * NTOK:((2 * p + 1) % 4 + 1) * NTOK]
                    v.drain()
                    v.tensor_tensor(olo, lo, cos_s[:, :], ALU.mult)
                    v.tensor_tensor(rtmp[:, :], hi, sin_s[:, :], ALU.mult)
                    v.drain()
                    v.tensor_tensor(olo, olo, rtmp[:, :], ALU.subtract)
                    v.tensor_tensor(ohi, hi, cos_s[:, :], ALU.mult)
                    v.drain()
                    v.tensor_tensor(rtmp[:, :], lo, sin_s[:, :], ALU.mult)
                    v.drain()
                    sq_r.inc(v.tensor_tensor(ohi, ohi, rtmp[:, :], ALU.add))

                for vc in range(32):
                    for j in range(8):
                        i = vc * 8 + j
                        v.wait_ge(sq_vt.h, i + 1)
                        if vc >= 2 and j == 0:
                            st_stgv.wait_group(v, vc - 2)
                        sq_ve.inc(v.tensor_copy(
                            vtev[:, (vc % 2) * NTOK + j * 128:(vc % 2) * NTOK + (j + 1) * 128],
                            (pvt0 if i % 2 == 0 else pvt1)[:, :]))

            @blk.sync
            def _(sy):
                if PHASES == "q2":
                    sy.wait_ge(sq_e.h, 96)
                    return
                for p in range(32):
                    sy.wait_ge(sq_r.h, p + 1)
                    what = "q" if p < 16 else "k"
                    hp = p % 16
                    olo = ropeb[:, (2 * p % 4) * NTOK:(2 * p % 4 + 1) * NTOK]
                    ohi = ropeb[:, ((2 * p + 1) % 4) * NTOK:((2 * p + 1) % 4 + 1) * NTOK]
                    for hloc in range(2):
                        h = 2 * hp + hloc
                        dst = sh1(a2a1qk_in, h // HL, h % HL, what)
                        st_stg.dma(sy.dma_start(out=dst[0:64, :],
                                                in_=olo[hloc * 64:hloc * 64 + 64, :]))
                        st_stg.dma(sy.dma_start(out=dst[64:128, :],
                                                in_=ohi[hloc * 64:hloc * 64 + 64, :]))
                    st_stg.end()
                for ve, vc in enumerate(VORD):
                    sy.wait_ge(sq_ve.h, (ve + 1) * 8)
                    vt = a2a1v1_in if vc % 4 < 2 else a2a1v2_in
                    dst = sh1(vt, vc // HL, (vc % HL) % 2, "v")
                    st_stgv.group(sy.dma_start(
                        out=dst.rearrange("(c p) d -> p c d", p=128),
                        in_=vtev[:, (ve % 2) * NTOK:(ve % 2 + 1) * NTOK].rearrange(
                            "p (c d) -> p c d", c=8)))

        if PHASES == "q2":
            return nc

        @blk.gpsimd
        def _(g):
            if PHASES == "q3":
                st_stg.wait_all(g)
                st_stgv.wait_all(g)
                return
            st_stg.wait_all(g)
            s_cc.inc(g.collective_compute(
                "AllToAll", ALU.bypass, replica_groups=[list(range(NCORES))],
                ins=[a2a1qk_in.ap().opt()], outs=[a2a1qk_out.ap().opt()]))
            st_stgv.wait_group(g, 15)
            s_cc.inc(g.collective_compute(
                "AllToAll", ALU.bypass, replica_groups=[list(range(NCORES))],
                ins=[a2a1v1_in.ap().opt()], outs=[a2a1v1_out.ap().opt()]))
            st_stgv.wait_all(g)
            s_cc.inc(g.collective_compute(
                "AllToAll", ALU.bypass, replica_groups=[list(range(NCORES))],
                ins=[a2a1v2_in.ap().opt()], outs=[a2a1v2_out.ap().opt()]))

        if PHASES == "q3":
            return nc
        if PHASES == "q":
            @blk.sync
            def _(sy):
                sy.wait_ge(s_cc.h, 3)
            return nc

        PISEQ = [b * HL + hh for hh in range(HL) for b in range(B)]
        PAIRS = []
        for o in range(8):
            pi = PISEQ[o]
            for Q in range(NQC):
                npr = 2 * (Q + 1)
                for pr in range(npr):
                    PAIRS.append((o, pi, Q, pr, npr, o * 8 + Q))
        NPAIR = len(PAIRS)
        pairs_through_gq = {}
        cnt = 0
        for (o, pi, Q, pr, npr, gq) in PAIRS:
            cnt += 1
            pairs_through_gq[gq] = cnt
        pairs_through_o = {o: pairs_through_gq[o * 8 + 7] for o in range(8)}

        with (
            nc.sbuf_tensor("qh", [128, 2 * S], BF16) as qh,
            nc.sbuf_tensor("kh", [128, 2 * S], BF16) as kh,
            nc.sbuf_tensor("vh", [128, 2 * S], BF16) as vh,
            nc.sbuf_tensor("expS", [128, 2 * 1024], BF16) as expS,
            nc.sbuf_tensor("recip", [128, 2 * 512], F32) as recip,
            nc.sbuf_tensor("part2", [1, 4 * 512], BF16) as part2,
            nc.sbuf_tensor("ctxev", [128, 2 * 512], BF16) as ctxev,
            nc.psum_tensor("psc", [128, 2 * 1024], F32) as psc,
            nc.psum_tensor("pctx", [128, 2 * 512], F32) as pctx,
            nc.psum_tensor("psm", [128, 2 * 512], F32) as psm,
        ):
            def emit_load(o):
                @blk.sync
                def _(sy):
                    pi = PISEQ[o]
                    b, hh = divmod(pi, HL)
                    if o == 0:
                        sy.wait_ge(s_cc.h, 1)
                    if o >= 2:
                        sy.wait_ge(sa_pv.h, pairs_through_o[o - 2])
                    sl = (o % 2)
                    for i in range(4):
                        rank = 4 * b + i
                        st_aqk.dma(sy.dma_start(out=qh[:, sl * S + i * NTOK:sl * S + (i + 1) * NTOK],
                                                in_=sh1(a2a1qk_out, rank, hh, "q")))
                        st_aqk.dma(sy.dma_start(out=kh[:, sl * S + i * NTOK:sl * S + (i + 1) * NTOK],
                                                in_=sh1(a2a1qk_out, rank, hh, "k")))
                    st_aqk.end()
                    if o == 0:
                        sy.wait_ge(s_cc.h, 2)
                    if o == 4:
                        sy.wait_ge(s_cc.h, 3)
                    vt = a2a1v1_out if hh < 2 else a2a1v2_out
                    for i in range(4):
                        rank = 4 * b + i
                        st_av.dma(sy.dma_start(
                            out=vh[:, sl * S + i * NTOK:sl * S + (i + 1) * NTOK].rearrange(
                                "p (c d) -> p c d", c=8),
                            in_=sh1(vt, rank, hh % 2, "v").rearrange("(c p) d -> p c d", p=128)))
                    st_av.end()

            def emit_stage(gq):
                @blk.sync
                def _(sy):
                    (o, Q) = divmod(gq, 8)
                    pi = PISEQ[o]
                    b, hh = divmod(pi, HL)
                    sy.wait_ge(sa_nm.h, gq + 1)
                    tens = a2a2a_in if hh < 2 else a2a2b_in
                    dst = sh2(hh, 4 * b + Q // 2)(tens)[:, (Q % 2) * 512:(Q % 2) * 512 + 512]
                    stt = st_asta if gq < 32 else st_astb
                    stt.group(sy.dma_start(out=dst,
                                           in_=ctxev[:, (gq % 2) * 512:(gq % 2 + 1) * 512]))

            emit_load(0)
            emit_load(1)
            for o in range(8):
                for Q in range(NQC):
                    emit_stage(o * 8 + Q)
                if o + 2 < 8:
                    emit_load(o + 2)

            @blk.tensor
            def _(t):
                def emit_scores(idx):
                    (o, pi, Q, pr, npr, gq) = PAIRS[idx]
                    sl = o % 2
                    if idx == 0 or PAIRS[idx - 1][0] != o:
                        st_aqk.wait_group(t, o)
                    if idx >= 2:
                        t.wait_ge(sa_ex.h, idx - 1)
                    diag = (pr >= npr - 2)
                    inst = None
                    for half in range(2):
                        kb = 2 * pr + half
                        dst = psc[:, (idx % 2) * 1024 + half * 512:(idx % 2) * 1024 + (half + 1) * 512]
                        inst = t.matmul(dst, kh[:, sl * S + kb * 128:sl * S + (kb + 1) * 128],
                                        qh[:, sl * S + Q * 512:sl * S + (Q + 1) * 512],
                                        start=True, stop=not diag)
                        if diag:
                            ti = kb - 4 * Q
                            inst = t.matmul(dst, ident[:, :], trimask[:, ti * 512:(ti + 1) * 512],
                                            start=False, stop=True)
                    sa_sc.inc(inst)

                def emit_pv(idx):
                    (o, pi, Q, pr, npr, gq) = PAIRS[idx]
                    sl = o % 2
                    t.wait_ge(sa_ex.h, idx + 1)
                    if idx == 0 or PAIRS[idx - 1][0] != o:
                        st_av.wait_group(t, o)
                    if pr == 0 and gq >= 2:
                        t.wait_ge(sa_nm.h, gq - 1)
                    inst = None
                    for half in range(2):
                        kb = 2 * pr + half
                        first = (pr == 0 and half == 0)
                        last = (pr == npr - 1 and half == 1)
                        ex = expS[:, (idx % 2) * 1024 + half * 512:(idx % 2) * 1024 + (half + 1) * 512]
                        t.matmul(pctx[:, (gq % 2) * 512:(gq % 2 + 1) * 512],
                                 vh[:, sl * S + kb * 128:sl * S + (kb + 1) * 128],
                                 ex, start=first, stop=last)
                        inst = t.matmul(psm[:, (gq % 2) * 512:(gq % 2 + 1) * 512],
                                        ones128[:, :], ex, start=first, stop=last)
                    sa_pv.inc(inst)

                for idx in range(NPAIR):
                    emit_scores(idx)
                    if idx >= 1:
                        emit_pv(idx - 1)
                emit_pv(NPAIR - 1)

            @blk.scalar
            def _(s):
                for idx in range(NPAIR):
                    s.wait_ge(sa_sc.h, idx + 1)
                    if idx == 0:
                        st_aqk.wait_group(s, 0)
                    if idx >= 2:
                        s.wait_ge(sa_pv.h, idx - 1)
                    sa_ex.inc(s.activation(expS[:, (idx % 2) * 1024:(idx % 2 + 1) * 1024],
                                           psc[:, (idx % 2) * 1024:(idx % 2 + 1) * 1024],
                                           AF.Exp, bias=EXP_SHIFT, scale=SM_SCALE))

            @blk.vector
            def _(v):
                for gq in range(64):
                    v.wait_ge(sa_pv.h, pairs_through_gq[gq])
                    if gq >= 2:
                        (st_asta if gq - 2 < 32 else st_astb).wait_group(v, (gq - 2) % 32)
                    v.drain()
                    v.reciprocal(recip[:, (gq % 2) * 512:(gq % 2 + 1) * 512],
                                 psm[:, (gq % 2) * 512:(gq % 2 + 1) * 512])
                    v.drain()
                    sa_nm.inc(v.tensor_tensor(ctxev[:, (gq % 2) * 512:(gq % 2 + 1) * 512],
                                              pctx[:, (gq % 2) * 512:(gq % 2 + 1) * 512],
                                              recip[:, (gq % 2) * 512:(gq % 2 + 1) * 512],
                                              ALU.mult))


        @blk.gpsimd
        def _(g):
            st_asta.wait_all(g)
            s_cc.inc(g.collective_compute(
                "AllToAll", ALU.bypass, replica_groups=[list(range(NCORES))],
                ins=[a2a2a_in.ap().opt()], outs=[a2a2a_out.ap().opt()]))
            st_astb.wait_all(g)
            s_cc.inc(g.collective_compute(
                "AllToAll", ALU.bypass, replica_groups=[list(range(NCORES))],
                ins=[a2a2b_in.ap().opt()], outs=[a2a2b_out.ap().opt()]))

        if PHASES == "a":
            @blk.sync
            def _(sy):
                sy.wait_ge(s_cc.h, 5)
            return nc

        with (
            nc.sbuf_tensor("attn_buf", [128, NSUB * E], BF16) as attn_buf,
            nc.sbuf_tensor("cxbuf", [128, 2 * 4096], BF16) as cxbuf,
            nc.sbuf_tensor("owbuf", [128, 2 * 2048], BF16) as owbuf,
            nc.sbuf_tensor("xbuf2", [128, 2 * E], BF16) as xbuf2,
            nc.sbuf_tensor("lnatmp", [128, E], BF16) as lnatmp,
            nc.sbuf_tensor("y1buf", [128, 2 * E], BF16) as y1buf,
            nc.sbuf_tensor("ln2tm", [128, E], BF16) as ln2tm,
            nc.sbuf_tensor("ln2sub", [128, 2 * 32 * 128], BF16) as ln2sub,
        ):
            with nc.psum_tensor("po", [128, 8 * 512], F32) as po:
                @blk.sync
                def _(sy):
                    sy.wait_ge(s_cc.h, 5)
                    for oc in range(8):
                        for fg in range(8):
                            i = oc * 8 + fg
                            if i >= 2:
                                sy.wait_ge(so_mt.h, i - 1)
                            base = (i % 2) * 4096
                            st_ocx.dma(sy.dma_start(
                                out=cxbuf[:, base:base + 2048].rearrange("p (hh t) -> p hh t", hh=2),
                                in_=a2a2a_out.ap()[fg:fg + 1, :].rearrange("o (hh p t) -> (o p) hh t", hh=2, p=128)))
                            st_ocx.dma(sy.dma_start(
                                out=cxbuf[:, base + 2048:base + 4096].rearrange("p (hh t) -> p hh t", hh=2),
                                in_=a2a2b_out.ap()[fg:fg + 1, :].rearrange("o (hh p t) -> (o p) hh t", hh=2, p=128)))
                            st_ocx.end()

                @blk.gpsimd
                def _(g):
                    g.wait_ge(s_cc.h, 5)
                    for oc in range(8):
                        for fg in range(8):
                            i = oc * 8 + fg
                            if i >= 2:
                                g.wait_ge(so_mt.h, i - 1)
                            st_ow.group(g.dma_start(
                                out=owbuf[:, (i % 2) * 2048:(i % 2 + 1) * 2048].rearrange(
                                    "p (fc c) -> p fc c", fc=4),
                                in_=out_w.ap()[fg * 512:(fg + 1) * 512, oc * 512:(oc + 1) * 512].rearrange(
                                    "(fc p) c -> p fc c", p=128)))

                @blk.tensor
                def _(t):
                    for oc in range(8):
                        inst = None
                        for fg in range(8):
                            i = oc * 8 + fg
                            st_ocx.wait_group(t, i)
                            st_ow.wait_group(t, i)
                            cb = (i % 2) * 4096
                            wb = (i % 2) * 2048
                            for fsub in range(4):
                                for sub in range(NSUB):
                                    if oc >= 1 and fg == 0 and fsub == 0:
                                        t.wait_ge(so_ev.h, (oc - 1) * NSUB + sub + 1)
                                    inst = t.matmul(
                                        po[:, sub * 512:(sub + 1) * 512],
                                        cxbuf[:, cb + fsub * 1024 + sub * 128:cb + fsub * 1024 + (sub + 1) * 128],
                                        owbuf[:, wb + fsub * 512:wb + (fsub + 1) * 512],
                                        start=(fg == 0 and fsub == 0), stop=False)
                            so_mt.inc(inst)
                        for sub in range(NSUB):
                            inst = t.matmul(po[:, sub * 512:(sub + 1) * 512],
                                            onesrow[:, :], outb_row[0:1, oc * 512:(oc + 1) * 512],
                                            start=False, stop=True)
                        so_mm.inc(inst)

                @blk.scalar
                def _(s):
                    for oc in range(8):
                        s.wait_ge(so_mm.h, oc + 1)
                        for sub in range(NSUB):
                            dst = attn_buf[:, :].rearrange("p (sb c) -> p sb c", sb=NSUB)[
                                :, sub:sub + 1, oc * 512:(oc + 1) * 512]
                            so_ev.inc(s.activation(dst, po[:, sub * 512:(sub + 1) * 512],
                                                   AF.Identity,
                                                   bias=np.float32(0.0), scale=F1))

            with (nc.psum_tensor("ptro0", [128, 128], BF16) as ptro0,
                  nc.psum_tensor("ptro1", [128, 128], BF16) as ptro1,
                  nc.psum_tensor("pg", [128, 64], F32) as pg,
                  nc.psum_tensor("plow", [128, 1024], F32) as plow,
                  nc.psum_tensor("pwt", [8, 128], BF16) as pwt,
                  nc.psum_tensor("pwrep", [128, 128], F32) as pwrep):

                def emit_x(s):
                    @blk.gpsimd
                    def _(g):
                        if s >= 2:
                            g.wait_ge(so_y1.h, s - 1)
                        st_ox.group(g.dma_start(out=xbuf2[:, (s % 2) * E:(s % 2 + 1) * E],
                                                in_=x_tok.ap()[s * 128:(s + 1) * 128, :]))

                def emit_spill(s):
                    @blk.sync
                    def _(sy):
                        sy.wait_ge(so_y1.h, s + 1)
                        st_y1sp.group(sy.dma_start(
                            out=y1_spill.ap()[s * 128:(s + 1) * 128, :],
                            in_=y1buf[:, (s % 2) * E:(s % 2 + 1) * E]))
                        sy.wait_ge(so_te.h, 32 * (s + 1))
                        st_ln2sp.group(sy.dma_start(
                            out=ln2_spill.ap().rearrange("(fc p) t -> p fc t", p=128)[:, :, s * 128:(s + 1) * 128],
                            in_=ln2sub[:, (s % 2) * E:(s % 2 + 1) * E].rearrange("p (fc c) -> p fc c", fc=32)))

                for s in range(NSUB):
                    emit_spill(s)
                for s in range(NSUB):
                    emit_x(s)

                def V1(s):
                    @blk.vector
                    def _(v):
                        if s == 0:
                            v.wait_ge(so_ev.h, 8 * NSUB)
                        v.drain()
                        ab = attn_buf[:, s * E:(s + 1) * E]
                        for ch in range(8):
                            v.bn_stats(stat6[:, 6 * ch:6 * ch + 6], ab[:, 512 * ch:512 * ch + 512])
                        v.drain()
                        v.bn_aggr(agg2[:, :], stat6[:, :])
                        v.drain()
                        v.tensor_copy(meanC[:, s:s + 1], agg2[:, 0:1])
                        so_v1.inc(v.tensor_copy(varC[:, s:s + 1], agg2[:, 1:2]))

                def S1(s):
                    @blk.scalar
                    def _(s_):
                        s_.wait_ge(so_v1.h, s + 1)
                        so_s1.inc(s_.activation(sqC[:, s:s + 1], varC[:, s:s + 1], AF.Sqrt,
                                                bias=np.float32(LN_EPS), scale=F1))

                def V2(s):
                    @blk.vector
                    def _(v):
                        v.wait_ge(so_s1.h, s + 1)
                        v.drain()
                        v.reciprocal(rstdC[:, s:s + 1], sqC[:, s:s + 1])
                        v.drain()
                        so_v2.inc(v.scalar_tensor_tensor(nmrC[:, s:s + 1], meanC[:, s:s + 1],
                                                         -1.0, rstdC[:, s:s + 1], ALU.mult, ALU.mult))

                def S2(s):
                    @blk.scalar
                    def _(s_):
                        s_.wait_ge(so_v2.h, s + 1)
                        if s >= 1:
                            s_.wait_ge(so_y1.h, s)
                        so_s2.inc(s_.activation(lnatmp[:, :], attn_buf[:, s * E:(s + 1) * E],
                                                AF.Identity, bias=nmrC[:, s:s + 1],
                                                scale=rstdC[:, s:s + 1]))

                def V34(s):
                    @blk.vector
                    def _(v):
                        v.wait_ge(so_s2.h, s + 1)
                        st_ox.wait_group(v, s)
                        if s >= 2:
                            st_y1sp.wait_group(v, s - 2)
                            v.wait_ge(so_s4.h, s - 1)
                        v.drain()
                        so_y1.inc(v.tensor_tensor(y1buf[:, (s % 2) * E:(s % 2 + 1) * E],
                                                  lnatmp[:, :],
                                                  xbuf2[:, (s % 2) * E:(s % 2 + 1) * E],
                                                  ALU.add))
                        v.drain()
                        yb = y1buf[:, (s % 2) * E:(s % 2 + 1) * E]
                        for ch in range(8):
                            v.bn_stats(stat6[:, 6 * ch:6 * ch + 6], yb[:, 512 * ch:512 * ch + 512])
                        v.drain()
                        v.bn_aggr(agg2[:, :], stat6[:, :])
                        v.drain()
                        v.tensor_copy(meanC[:, 8 + s:9 + s], agg2[:, 0:1])
                        so_v4.inc(v.tensor_copy(varC[:, 8 + s:9 + s], agg2[:, 1:2]))

                def S3(s):
                    @blk.scalar
                    def _(s_):
                        s_.wait_ge(so_v4.h, s + 1)
                        so_s3.inc(s_.activation(sqC[:, 8 + s:9 + s], varC[:, 8 + s:9 + s], AF.Sqrt,
                                                bias=np.float32(LN_EPS), scale=F1))

                def V5(s):
                    @blk.vector
                    def _(v):
                        v.wait_ge(so_s3.h, s + 1)
                        v.drain()
                        v.reciprocal(rstdC[:, 8 + s:9 + s], sqC[:, 8 + s:9 + s])
                        v.drain()
                        so_v5.inc(v.scalar_tensor_tensor(nmrC[:, 8 + s:9 + s], meanC[:, 8 + s:9 + s],
                                                         -1.0, rstdC[:, 8 + s:9 + s], ALU.mult, ALU.mult))

                def S4(s):
                    @blk.scalar
                    def _(s_):
                        s_.wait_ge(so_v5.h, s + 1)
                        if s >= 1:
                            s_.wait_ge(so_tr.h, 32 * s)
                        so_s4.inc(s_.activation(ln2tm[:, :], y1buf[:, (s % 2) * E:(s % 2 + 1) * E],
                                                AF.Identity, bias=nmrC[:, 8 + s:9 + s],
                                                scale=rstdC[:, 8 + s:9 + s]))

                def T1G(s):
                    @blk.tensor
                    def _(t):
                        t.wait_ge(so_s4.h, s + 1)
                        for f in range(32):
                            i = s * 32 + f
                            if i >= 2:
                                t.wait_ge(so_te.h, i - 1)
                            so_tr.inc(t.transpose((ptro0 if i % 2 == 0 else ptro1)[:, :],
                                                  ln2tm[:, f * 128:(f + 1) * 128], ident[:, :]))
                        t.wait_ge(so_te.h, 32 * (s + 1))
                        l2s = ln2sub[:, (s % 2) * E:(s % 2 + 1) * E]
                        inst = None
                        for fc in range(32):
                            inst = t.matmul(pg[:, s * 8:(s + 1) * 8],
                                            l2s[:, fc * 128:(fc + 1) * 128],
                                            gwT_s[:, fc * 8:(fc + 1) * 8],
                                            start=(s == 0 and fc == 0),
                                            stop=(s == NSUB - 1 and fc == 31),
                                            skip_group_check=True)
                        for fc in range(32):
                            inst = t.matmul(plow[:, s * 128:(s + 1) * 128],
                                            aT_s[:, fc * 128:(fc + 1) * 128],
                                            l2s[:, fc * 128:(fc + 1) * 128],
                                            start=(s % 4 == 0 and fc == 0),
                                            stop=(s % 4 == 3 and fc == 31),
                                            skip_group_check=True)
                        sf_g.inc(inst)

                def V6(s):
                    @blk.vector
                    def _(v):
                        for f in range(32):
                            i = s * 32 + f
                            v.wait_ge(so_tr.h, i + 1)
                            if f == 0 and s >= 2:
                                st_ln2sp.wait_group(v, s - 2)
                                v.wait_ge(sf_g.h, s - 1)
                            so_te.inc(v.tensor_copy(
                                ln2sub[:, (s % 2) * E + f * 128:(s % 2) * E + (f + 1) * 128],
                                (ptro0 if i % 2 == 0 else ptro1)[:, :]))

                for tau in range(NSUB + 5):
                    if 0 <= tau - 4 < NSUB:
                        S4(tau - 4)
                        T1G(tau - 4)
                    if 0 <= tau - 3 < NSUB:
                        S3(tau - 3)
                    if 0 <= tau - 2 < NSUB:
                        S2(tau - 2)
                    if tau < NSUB:
                        V1(tau)
                        S1(tau)
                    if 0 <= tau - 1 < NSUB:
                        V2(tau - 1)
                    if 0 <= tau - 2 < NSUB:
                        V34(tau - 2)
                    if 0 <= tau - 3 < NSUB:
                        V5(tau - 3)
                        V6(tau - 3)

                for s in range(NSUB):
                    @blk.vector
                    def _(v, s=s):
                        v.wait_ge(sf_g.h, NSUB)
                        if s >= 1:
                            v.wait_ge(sf_wv.h, s)
                        v.drain()
                        v.tensor_copy(gl_s[:, :], pg[:, s * 8:(s + 1) * 8])
                        v.drain()
                        v.reduce_max(m1n_s[:, :], gl_s[:, :], axis=AX, negate=True)
                        v.drain()
                        v.tensor_scalar(msk_s[:, :], gl_s[:, :], m1n_s[:, :], None, ALU.add)
                        v.drain()
                        v.tensor_scalar(keep_s[:, :], msk_s[:, :], 0.0, None, ALU.is_ge)
                        v.drain()
                        v.scalar_tensor_tensor(msk_s[:, :], keep_s[:, :], -1e30,
                                               msk_s[:, :], ALU.mult, ALU.add)
                        v.drain()
                        hv.inc(v.reduce_max(m2_s[:, :], msk_s[:, :], axis=AX))

                    @blk.scalar
                    def _(s_, s=s):
                        s_.wait_ge(hv.h, hv.n)
                        hsc.inc(s_.activation(expl_s[:, :], gl_s[:, :], AF.Exp,
                                              bias=m1n_s[:, :], scale=F1))

                    @blk.vector
                    def _(v, s=s):
                        v.wait_ge(hsc.h, hsc.n)
                        v.drain()
                        v.tensor_scalar(msk_s[:, :], gl_s[:, :], m1n_s[:, :], None, ALU.add)
                        v.drain()
                        v.tensor_scalar(keep_s[:, :], msk_s[:, :], m2_s[:, :], None, ALU.is_ge)
                        v.drain()
                        v.tensor_tensor(expl_s[:, :], expl_s[:, :], keep_s[:, :], ALU.mult)
                        v.drain()
                        v.reduce_sum(wsum_s[:, :], expl_s[:, :], axis=AX)
                        v.drain()
                        v.reciprocal(rws_s[:, :], wsum_s[:, :])
                        v.drain()
                        hv.inc(v.tensor_scalar(wful_s[:, :], expl_s[:, :], rws_s[:, :],
                                               None, ALU.mult))

                    @blk.tensor
                    def _(t, s=s):
                        t.wait_ge(hv.h, hv.n)
                        ht.inc(t.transpose(pwt[:, :], wful_s[:, :], ident[:, :]))

                    @blk.vector
                    def _(v, s=s):
                        v.wait_ge(ht.h, ht.n)
                        hv.inc(v.tensor_copy(wfT_s[:, :], pwt[:, :]))

                    @blk.tensor
                    def _(t, s=s):
                        t.wait_ge(hv.h, hv.n)
                        ht.inc(t.matmul(pwrep[:, :], rep_s[:, :], wfT_s[:, :],
                                        start=True, stop=True))

                    @blk.vector
                    def _(v, s=s):
                        v.wait_ge(ht.h, ht.n)
                        sf_wv.inc(v.tensor_copy(wrep_s[:, s * 128:(s + 1) * 128], pwrep[:, :]))

                @blk.scalar
                def _(s_):
                    s_.wait_ge(sf_g.h, NSUB)
                    hsc.inc(s_.activation(lowT_s[:, :], plow[:, :], AF.Identity,
                                          bias=np.float32(0.0), scale=F1))
                HSC_LOWT = hsc.n

                @blk.vector
                def _(v):
                    v.wait_ge(sf_wv.h, NSUB)
                    v.wait_ge(hsc.h, HSC_LOWT)
                    v.drain()
                    sf_low.inc(v.tensor_tensor(lowW_s[:, :], lowT_s[:, :], wrep_s[:, :],
                                               ALU.mult))

        if PHASES == "o":
            @blk.sync
            def _(sy):
                st_y1sp.wait_all(sy)
                st_ln2sp.wait_all(sy)
            return nc

        fes = ExitStack()
        with fes:
            FSB = lambda n, sh, dt: fes.enter_context(nc.sbuf_tensor(n, list(sh), dt))
            zbuf = FSB("zbuf", [128, NSUB * E], BF16)
            l2buf = FSB("l2buf", [128, 2 * 4096], BF16)
            bwbuf = FSB("bwbuf", [128, 2 * 2048], BF16)
            zsum_s = FSB("zsum_s", [128, 64], F32)
            zsq_s = FSB("zsq_s", [128, 64], F32)
            zdump = FSB("zdump", [128, 512], BF16)
            y1ld = FSB("y1ld", [128, 2 * E], BF16)
            lnztmp = FSB("lnztmp", [128, E], F32)
            outbuf = FSB("outbuf", [128, 2 * E], F32)
            zr_s = FSB("zr_s", [128, 2], F32)
            with nc.psum_tensor("pz", [128, 8 * 512], F32) as pz:
                @blk.sync
                def _(sy):
                    st_y1sp.wait_all(sy)
                    st_ln2sp.wait_all(sy)
                    for oc in range(8):
                        for fg in range(8):
                            i = oc * 8 + fg
                            if i >= 2:
                                sy.wait_ge(sf_mt.h, i - 1)
                            st_fl2c.group(sy.dma_start(
                                out=l2buf[:, (i % 2) * 4096:(i % 2 + 1) * 4096].rearrange(
                                    "p (fc t) -> p fc t", fc=4),
                                in_=ln2_spill.ap()[fg * 512:(fg + 1) * 512, :].rearrange(
                                    "(fc p) t -> p fc t", p=128)))

                @blk.gpsimd
                def _(g):
                    st_y1sp.wait_all(g)
                    st_ln2sp.wait_all(g)
                    for oc in range(8):
                        for fg in range(8):
                            i = oc * 8 + fg
                            if i >= 2:
                                g.wait_ge(sf_mt.h, i - 1)
                            st_fw.group(g.dma_start(
                                out=bwbuf[:, (i % 2) * 2048:(i % 2 + 1) * 2048].rearrange(
                                    "p (fc c) -> p fc c", fc=4),
                                in_=base_wT.ap()[fg * 512:(fg + 1) * 512, oc * 512:(oc + 1) * 512].rearrange(
                                    "(fc p) c -> p fc c", p=128)))

                @blk.tensor
                def _(t):
                    t.wait_ge(sf_low.h, 1)
                    for oc in range(8):
                        if oc >= 1:
                            t.wait_ge(sf_ev.h, oc)
                        inst = None
                        for fg in range(8):
                            i = oc * 8 + fg
                            st_fl2c.wait_group(t, i)
                            st_fw.wait_group(t, i)
                            lb = (i % 2) * 4096
                            wb = (i % 2) * 2048
                            for fsub in range(4):
                                for sub in range(NSUB):
                                    inst = t.matmul(
                                        pz[:, sub * 512:(sub + 1) * 512],
                                        l2buf[:, lb + fsub * 1024 + sub * 128:lb + fsub * 1024 + (sub + 1) * 128],
                                        bwbuf[:, wb + fsub * 512:wb + (fsub + 1) * 512],
                                        start=(fg == 0 and fsub == 0), stop=False)
                            sf_mt.inc(inst)
                        for sub in range(NSUB):
                            t.matmul(pz[:, sub * 512:(sub + 1) * 512],
                                     lowW_s[:, sub * 128:(sub + 1) * 128],
                                     bB_s[:, oc * 512:(oc + 1) * 512],
                                     start=False, stop=False)
                        for sub in range(NSUB):
                            inst = t.matmul(pz[:, sub * 512:(sub + 1) * 512],
                                            onesrow[:, :], baseb_row[0:1, oc * 512:(oc + 1) * 512],
                                            start=False, stop=True)
                        sf_mm.inc(inst)

                @blk.scalar
                def _(s):
                    for oc in range(8):
                        s.wait_ge(sf_mm.h, oc + 1)
                        inst = None
                        for sub in range(NSUB):
                            s.activation(zbuf[:, sub * E + oc * 512:sub * E + (oc + 1) * 512],
                                         pz[:, sub * 512:(sub + 1) * 512], AF.Identity,
                                         bias=np.float32(0.0), scale=F1,
                                         accum_out=zsum_s[:, sub * 8 + oc:sub * 8 + oc + 1])
                            s.drain()
                            inst = s.activation(zdump[:, :],
                                                pz[:, sub * 512:(sub + 1) * 512], AF.Square,
                                                bias=np.float32(0.0), scale=F1,
                                                accum_out=zsq_s[:, sub * 8 + oc:sub * 8 + oc + 1])
                        sf_ev.inc(inst)

            if True:
                for sub in range(NSUB):
                    @blk.sync
                    def _(sy, sub=sub):
                        if sub >= 2:
                            sy.wait_ge(sf_o.h, sub - 1)
                        st_fy1.group(sy.dma_start(out=y1ld[:, (sub % 2) * E:(sub % 2 + 1) * E],
                                                  in_=y1_spill.ap()[sub * 128:(sub + 1) * 128, :]))

                    @blk.vector
                    def _(v, sub=sub):
                        if sub == 0:
                            v.wait_ge(sf_ev.h, 8)
                        v.drain()
                        v.reduce_sum(zr_s[:, 0:1], zsum_s[:, sub * 8:(sub + 1) * 8], axis=AX)
                        v.reduce_sum(zr_s[:, 1:2], zsq_s[:, sub * 8:(sub + 1) * 8], axis=AX)
                        v.drain()
                        v.tensor_scalar(mean_s[:, :], zr_s[:, 0:1], float(1.0 / E), None, ALU.mult)
                        v.tensor_scalar(var_s[:, :], zr_s[:, 1:2], float(1.0 / E), None, ALU.mult)
                        v.drain()
                        v.tensor_tensor(zr_s[:, 0:1], mean_s[:, :], mean_s[:, :], ALU.mult)
                        v.drain()
                        hv.inc(v.tensor_tensor(var_s[:, :], var_s[:, :], zr_s[:, 0:1], ALU.subtract))

                    @blk.scalar
                    def _(s, sub=sub):
                        s.wait_ge(hv.h, hv.n)
                        hsc.inc(s.activation(sqv_s[:, :], var_s[:, :], AF.Sqrt,
                                             bias=np.float32(LN_EPS), scale=F1))

                    @blk.vector
                    def _(v, sub=sub):
                        v.wait_ge(hsc.h, hsc.n)
                        v.reciprocal(rstd_s[:, :], sqv_s[:, :])
                        v.drain()
                        hv.inc(v.scalar_tensor_tensor(nmr_s[:, :], mean_s[:, :], -1.0,
                                                      rstd_s[:, :], ALU.mult, ALU.mult))

                    @blk.scalar
                    def _(s, sub=sub):
                        s.wait_ge(hv.h, hv.n)
                        hsc.inc(s.activation(lnztmp[:, :], zbuf[:, sub * E:(sub + 1) * E],
                                             AF.Identity, bias=nmr_s[:, :], scale=rstd_s[:, :]))

                    @blk.vector
                    def _(v, sub=sub):
                        v.wait_ge(hsc.h, hsc.n)
                        st_fy1.wait_group(v, sub)
                        if sub >= 2:
                            st_fst.wait_group(v, sub - 2)
                        sf_o.inc(v.tensor_tensor(outbuf[:, (sub % 2) * E:(sub % 2 + 1) * E],
                                                 lnztmp[:, :],
                                                 y1ld[:, (sub % 2) * E:(sub % 2 + 1) * E],
                                                 ALU.add))

                    @blk.sync
                    def _(sy, sub=sub):
                        sy.wait_ge(sf_o.h, sub + 1)
                        st_fst.group(sy.dma_start(out=y_out.ap()[sub * 128:(sub + 1) * 128, :],
                                                  in_=outbuf[:, (sub % 2) * E:(sub % 2 + 1) * E]))

                @blk.sync
                def _(sy):
                    st_fst.wait_all(sy)

    return nc


def _host_prep(L0_attn_hidden_states, L0_attn_qkv_w, L0_attn_qkv_b, L0_attn_out_w,
               L0_attn_out_b, L0_ffn_gate_w, L0_ffn_base_w, L0_ffn_base_b,
               L0_ffn_lora_A, L0_ffn_lora_B):
    x = np.ascontiguousarray(np.asarray(L0_attn_hidden_states, dtype=np.float32).reshape(B * S, E))
    qkv_w = np.asarray(L0_attn_qkv_w, dtype=np.float32)
    qkv_b = np.asarray(L0_attn_qkv_b, dtype=np.float32)

    perm = []
    for p in range(16):
        h0, h1 = 2 * p, 2 * p + 1
        perm += list(range(h0 * 128, h0 * 128 + 64)) + list(range(h1 * 128, h1 * 128 + 64))
        perm += list(range(h0 * 128 + 64, h0 * 128 + 128)) + list(range(h1 * 128 + 64, h1 * 128 + 128))
    perm = np.asarray(perm)
    qkv_w_p = np.empty_like(qkv_w)
    qkv_w_p[:, 0:E] = qkv_w[:, 0:E][:, perm]
    qkv_w_p[:, E:2 * E] = qkv_w[:, E:2 * E][:, perm]
    qkv_w_p[:, 2 * E:] = qkv_w[:, 2 * E:]
    qkv_b_p = np.empty_like(qkv_b)
    qkv_b_p[0:E] = qkv_b[0:E][perm]
    qkv_b_p[E:2 * E] = qkv_b[E:2 * E][perm]
    qkv_b_p[2 * E:] = qkv_b[2 * E:]
    qkv_b_p = np.ascontiguousarray(qkv_b_p.reshape(96, 128).T)

    out_w = np.ascontiguousarray(np.asarray(L0_attn_out_w, dtype=np.float32))
    out_b = np.asarray(L0_attn_out_b, dtype=np.float32)
    base_wT = np.ascontiguousarray(np.asarray(L0_ffn_base_w, dtype=np.float32).T)
    base_b = np.asarray(L0_ffn_base_b, dtype=np.float32)
    gate_wT = np.ascontiguousarray(np.asarray(L0_ffn_gate_w, dtype=np.float32).T)
    a_allT = np.ascontiguousarray(np.asarray(L0_ffn_lora_A, dtype=np.float32).transpose(2, 0, 1).reshape(E, 128))
    big_bT = np.ascontiguousarray(np.asarray(L0_ffn_lora_B, dtype=np.float32).transpose(0, 2, 1).reshape(128, E))

    ident = np.eye(128, dtype=np.float32)
    ones = np.ones((128, 128), dtype=np.float32)
    rep = np.zeros((8, 128), dtype=np.float32)
    for e in range(8):
        rep[e, e * 16:(e + 1) * 16] = 2.0
    trimask = np.zeros((128, 4 * 512), dtype=np.float32)
    for ti in range(4):
        kl = np.arange(128)[:, None]
        ql = np.arange(512)[None, :]
        trimask[:, ti * 512:(ti + 1) * 512] = np.where(128 * ti + kl > ql, np.float32(MASK_VAL), 0.0)

    inv_freq = 1.0 / (ROPE_BASE_F ** (np.arange(64, dtype=np.float64) / 64.0))
    in_maps = []
    shared = dict(qkv_w=qkv_w_p, qkv_b=qkv_b_p, out_w=out_w, out_b=out_b,
                  base_wT=base_wT, base_b=base_b, gate_wT=gate_wT,
                  a_allT=a_allT, big_bT=big_bT, ident_in=ident,
                  trimask_in=trimask, ones_in=ones, rep_in=rep)
    for c in range(NCORES):
        pos = (np.arange(NTOK, dtype=np.int64) + NTOK * c) % S
        ang = pos[:, None].astype(np.float64) * inv_freq[None, :]
        cosT = np.cos(ang).T.astype(np.float32)
        sinT = np.sin(ang).T.astype(np.float32)
        m = dict(shared)
        m["x_tok"] = np.ascontiguousarray(x[c * NTOK:(c + 1) * NTOK])
        m["cos2"] = np.ascontiguousarray(np.vstack([cosT, cosT]))
        m["sin2"] = np.ascontiguousarray(np.vstack([sinT, sinT]))
        in_maps.append(m)
    return in_maps


ROPE_BASE_F = 10000.0

_NC_CACHE = []


def kernel(**inputs):
    in_maps = _host_prep(**inputs)
    if not _NC_CACHE:
        _NC_CACHE.append(build_program())
    nc = _NC_CACHE[0]
    res = run_bass_kernel_spmd(nc, in_maps, core_ids=list(range(NCORES)))
    y = np.concatenate([np.asarray(r["y_out"]) for r in res.results], axis=0)
    return y.reshape(B, S, E)
